# version 12
# baseline (speedup 1.0000x reference)
import contextlib
import numpy as np
import ml_dtypes
import concourse.bass as bass
import concourse.mybir as mybir
from concourse.bass_utils import run_bass_kernel_spmd

F32 = mybir.dt.float32
BF16 = mybir.dt.bfloat16
AF = mybir.ActivationFunctionType
ALU = mybir.AluOpType
AX = mybir.AxisListType
bfdt = ml_dtypes.bfloat16

S = 2048
DM = 2048
H = 32
HF = 5632
NK = DM // 128
NSC = S // 128
NPAIR = H // 2
NHF = HF // 128
QEPS = 1e-8
LNEPS = 1e-5

ENGINES = ("sync", "gpsimd", "scalar", "vector", "tensor")
CONST_VALS = [0.0, 1.0, -1.0, LNEPS, 0.125, 0.5, 2.0, 3.0]


class Prog:

    def __init__(self):
        self.ops = {e: [] for e in ENGINES}
        self.cnt = {}
        self.psum_dirty = {}
        self.pe_sat = None
        self.last_reader = {}

    def _bump(self, sem, by):
        c = self.cnt.get(sem.name, 0) + by
        self.cnt[sem.name] = c
        return (sem, c)

    def op(self, eng, emit, deps=(), inc=None, inc_by=1, psum_w=None, psum_r=None):
        deps = [d for d in deps if d is not None]
        if eng == "tensor" and psum_w is not None:
            lr = self.last_reader.pop(psum_w, None)
            if lr is not None:
                deps.append(lr)
            if self.psum_dirty.get(psum_w) and not deps:
                assert self.pe_sat is not None, "matmul separator unavailable"
                deps.append(self.pe_sat)
        if eng == "tensor" and deps:
            for k in self.psum_dirty:
                self.psum_dirty[k] = False
            self.pe_sat = deps[-1]
        if eng == "tensor" and psum_w is not None:
            self.psum_dirty[psum_w] = True
        dd = {}
        for (sem, val) in deps:
            if val > dd.get(sem.name, (None, -1))[1]:
                dd[sem.name] = (sem, val)
        deps = tuple(dd.values())
        ev = None
        if inc is not None:
            ev = self._bump(inc, inc_by)
        if psum_r is not None:
            assert ev is not None
            for nm in (psum_r if isinstance(psum_r, (list, tuple)) else [psum_r]):
                self.last_reader[nm] = ev
        self.ops[eng].append((emit, deps, ((inc, inc_by),) if inc is not None else ()))
        return ev

    def emit(self, nc, preamble=None):
        with nc.Block() as blk:
            for eng_name in ENGINES:
                oplist = self.ops[eng_name]
                if preamble is not None and eng_name != "sync":
                    oplist = [(None, (preamble,), ())] + oplist

                def make(oplist):
                    def fn(eng):
                        for (emit, deps, incs) in oplist:
                            for (sem, val) in deps:
                                eng.wait_ge(sem, val)
                            if emit is None:
                                continue
                            inst = emit(eng)
                            for (sem, by) in incs:
                                inst.then_inc(sem, by)
                    return fn

                getattr(blk, eng_name)(make(oplist))


def build_nc(debug=False):
    nc = bass.Bass("TRN2", target_bir_lowering=False,
                   dynamic_dma_scratch_size=16384)

    X = nc.dram_tensor("x", [S, DM], F32, kind="ExternalInput").ap()
    WQG = nc.dram_tensor("wqg", [NPAIR, 128, NK, 128], F32, kind="ExternalInput").ap()
    WKG = nc.dram_tensor("wkg", [NPAIR, 128, NK, 128], F32, kind="ExternalInput").ap()
    WVG = nc.dram_tensor("wvg", [NPAIR, 128, NK, 128], F32, kind="ExternalInput").ap()
    WFT = nc.dram_tensor("wft", [DM, H], F32, kind="ExternalInput").ap()
    BF = nc.dram_tensor("bfv", [1, H], F32, kind="ExternalInput").ap()
    WOT = nc.dram_tensor("wot", [DM, DM], F32, kind="ExternalInput").ap()
    W1G = nc.dram_tensor("w1g", [NHF, 2, 128, 8, 128], F32, kind="ExternalInput").ap()
    W2G = nc.dram_tensor("w2g", [NHF, 2, 128, 8, 128], F32, kind="ExternalInput").ap()
    W3G = nc.dram_tensor("w3g", [4, 22, 128, 2, 512], F32, kind="ExternalInput").ap()
    IDB = nc.dram_tensor("idb", [128, 128], BF16, kind="ExternalInput").ap()
    IDF = nc.dram_tensor("idf", [128, 128], F32, kind="ExternalInput").ap()
    TRIU = nc.dram_tensor("triu", [128, 128], BF16, kind="ExternalInput").ap()
    NEGM = nc.dram_tensor("negm", [128, 128], F32, kind="ExternalInput").ap()
    CONSTV = nc.dram_tensor("constv", [128, 8], F32, kind="ExternalInput").ap()
    Y = nc.dram_tensor("y", [S, DM], F32, kind="ExternalOutput").ap()
    dbg = dict(kind="ExternalOutput") if debug else {}
    ATTN = nc.dram_tensor("attn_scr", [S, DM], F32, **dbg).ap()
    FCT = nc.dram_tensor("fct_scr", [H, S], F32, **dbg).ap()
    RDN = nc.dram_tensor("rdn_scr", [H, S], F32, **dbg).ap()
    GP = nc.dram_tensor("gp_scr", [3, 128], F32).ap()
    TAU = nc.dram_tensor("tau_scr", [1, 4], F32).ap()
    Q1D = nc.dram_tensor("q1_scr", [NHF, 128, 2048], BF16).ap()
    Q2D = nc.dram_tensor("q2_scr", [NHF, 128, 2048], BF16).ap()
    Q3D = nc.dram_tensor("q3_scr", [4, 11, 128, 2048], BF16).ap()

    P = Prog()
    st = contextlib.ExitStack()
    E = st.enter_context

    sDs = E(nc.semaphore("sDs"))
    sDg = E(nc.semaphore("sDg"))
    sDc = E(nc.semaphore("sDc"))
    sPE = E(nc.semaphore("sPE"))
    sA = E(nc.semaphore("sA"))
    sV = E(nc.semaphore("sV"))
    sG = E(nc.semaphore("sG"))
    sOUT = E(nc.semaphore("sOUT"))

    SEMS = {}

    def stream_sem(name):
        if name not in SEMS:
            SEMS[name] = E(nc.semaphore("sdma_" + name))
        return SEMS[name]

    def dma(q, out, in_, deps=(), stream=None, slow=False):
        assert stream is not None
        s = stream_sem(stream)

        def em(e, out=out, in_=in_, slow=slow):
            if slow:
                return e.dma_start(out=out, in_=in_, allow_slow_non_contiguous=True)
            return e.dma_start(out=out, in_=in_)
        return P.op(q, em, deps=deps, inc=s, inc_by=16)

    def mm(lhsT, rhs, out, start, stop, psum, deps=(), inc=None):
        def em(e, out=out, lhsT=lhsT, rhs=rhs, start=start, stop=stop):
            return e.matmul(out, lhsT, rhs, start=start, stop=stop)
        return P.op("tensor", em, deps=deps, inc=inc, psum_w=psum)

    def transp(out, in_, ident, psum, deps=(), inc=sPE):
        def em(e, out=out, in_=in_, ident=ident):
            return e.transpose(out, in_, ident)
        return P.op("tensor", em, deps=deps, inc=inc, psum_w=psum)

    def act(out, in_, func, bias=0.0, scale=1.0, deps=(), psum_r=None):
        def em(e, out=out, in_=in_, func=func, bias=bias, scale=scale):
            return e.activation(out=out, in_=in_, func=func, bias=bias, scale=scale)
        return P.op("scalar", em, deps=deps, inc=sA, psum_r=psum_r)

    def vop(name, *args, deps=(), psum_r=None, eng="vector", **kw):
        def em(e, name=name, args=args, kw=kw):
            return getattr(e, name)(*args, **kw)
        return P.op(eng, em, deps=deps, inc=(sG if eng == "gpsimd" else sV),
                    psum_r=psum_r)

    xnT1 = None
    xnT2 = [E(nc.sbuf_tensor(f"xnT2_{k}", [128, 1024], BF16)) for k in range(NK)]
    identb = E(nc.sbuf_tensor("identb", [128, 128], BF16))
    triu = E(nc.sbuf_tensor("triu_sb", [128, 128], BF16))
    negm = E(nc.sbuf_tensor("negm_sb", [128, 128], F32))
    taubc = [E(nc.sbuf_tensor(f"taubc{w}", [128, 1], F32)) for w in range(3)]
    taubcn = [E(nc.sbuf_tensor(f"taubcn{w}", [128, 1], F32)) for w in range(3)]
    constv = E(nc.sbuf_tensor("constv_sb", [128, 8], F32))
    phg = contextlib.ExitStack()
    Eg = phg.enter_context
    gbuf = [Eg(nc.sbuf_tensor(f"gbuf{i}", [128, 512], F32, side="right")) for i in range(2)]
    gpart = [Eg(nc.sbuf_tensor(f"gpart{i}", [128, 1], F32, side="right"))
             for i in range(2)]
    gaccr = Eg(nc.sbuf_tensor("gaccr", [128, 1], F32, side="right"))
    gsum = Eg(nc.sbuf_tensor("gsum", [1, 1], F32, side="right"))
    phnf = contextlib.ExitStack()
    negF = [phnf.enter_context(nc.sbuf_tensor(f"negF{j}", [128, 32], F32))
            for j in range(NSC)]
    phot = contextlib.ExitStack()
    OT = [phot.enter_context(nc.sbuf_tensor(f"OT{c}", [128, 2048], BF16, side="right"))
          for c in range(NPAIR)]
    phx1 = contextlib.ExitStack()

    psm = [E(nc.psum_tensor(f"psm{i}", [128, 512], F32)) for i in range(4)]
    psO = E(nc.psum_tensor("psO", [65, 2048], F32))
    PSM = [f"psm{i}" for i in range(4)]

    xnT1 = [phx1.enter_context(nc.sbuf_tensor(f"xnT1_{k}", [128, 1024], BF16))
            for k in range(NK)]

    e_constv = dma("sync", constv[:, :], CONSTV, stream="const")
    for i, v in enumerate(CONST_VALS):
        try:
            nc.const_aps.register(F32, v, constv[:, i:i + 1])
        except Exception:
            pass
    dma("sync", identb[:, :], IDB, stream="const")
    dma("sync", triu[:, :], TRIU, stream="const")
    dma("sync", negm[:, :], NEGM, stream="const")

    ph0 = contextlib.ExitStack()
    E0 = ph0.enter_context
    NB0 = 3
    identf = E0(nc.sbuf_tensor("identf", [128, 128], F32, side="right"))
    xbuf = [E0(nc.sbuf_tensor(f"xbuf{i}", [128, 2048], F32, side="right")) for i in range(NB0)]
    sqbuf = E0(nc.sbuf_tensor("sqbuf", [128, 2048], F32, side="right"))
    xnbuf = [E0(nc.sbuf_tensor(f"xnbuf{i}", [128, 2048], BF16, side="right")) for i in range(NB0)]
    s1t = [E0(nc.sbuf_tensor(f"s1t{i}", [128, 1], F32, side="right")) for i in range(NB0)]
    s2t = [E0(nc.sbuf_tensor(f"s2t{i}", [128, 1], F32, side="right")) for i in range(NB0)]
    mt = [E0(nc.sbuf_tensor(f"mt{i}", [128, 1], F32, side="right")) for i in range(NB0)]
    m2t = [E0(nc.sbuf_tensor(f"m2t{i}", [128, 1], F32, side="right")) for i in range(NB0)]
    vart = [E0(nc.sbuf_tensor(f"vart{i}", [128, 1], F32, side="right")) for i in range(NB0)]
    sdt = [E0(nc.sbuf_tensor(f"sdt{i}", [128, 1], F32, side="right")) for i in range(NB0)]
    rstdt = [E0(nc.sbuf_tensor(f"rstdt{i}", [128, 1], F32, side="right")) for i in range(NB0)]
    nmrs = [E0(nc.sbuf_tensor(f"nmrs{i}", [128, 1], F32, side="right")) for i in range(NB0)]
    ftile = [E0(nc.sbuf_tensor(f"ftile{i}", [128, 32], F32, side="right")) for i in range(NSC)]
    fT = E0(nc.sbuf_tensor("fT", [32, 2048], F32, side="right"))
    FcumT = E0(nc.sbuf_tensor("FcumT", [32, 2048], F32, side="right"))
    ubuf = [E0(nc.sbuf_tensor(f"ubuf{i}", [128, 32], F32, side="right")) for i in range(2)]
    abuf = [E0(nc.sbuf_tensor(f"abuf{i}", [128, 32], F32, side="right")) for i in range(2)]
    ebuf = [E0(nc.sbuf_tensor(f"ebuf{i}", [128, 32], F32, side="right")) for i in range(2)]
    lbuf = [E0(nc.sbuf_tensor(f"lbuf{i}", [128, 32], F32, side="right")) for i in range(2)]
    mnbuf = [E0(nc.sbuf_tensor(f"mnbuf{i}", [128, 32], F32, side="right")) for i in range(2)]
    wfb = E0(nc.sbuf_tensor("wfb", [128, 512], BF16, side="right"))
    bfbc = E0(nc.sbuf_tensor("bfbc", [128, 32], F32, side="right"))

    dma("sync", identf[:, :], IDF, stream="const")
    e_wfb = dma("gpsimd", wfb[:, :].rearrange("p (k h) -> p k h", k=NK),
                WFT.rearrange("(k p) h -> p k h", p=128), stream="wfb")
    E_CONST = dma("sync", bfbc[:, :], BF.broadcast_to([128, H]), stream="const")
    e_bfbc = E_CONST
    e_idb = e_idf = e_triu = e_negm = E_CONST

    pstv = [psm[2][:, 0:64].bitcast(BF16), psm[3][:, 0:64].bitcast(BF16)]
    pftv = [psm[2][0:32, 128:256], psm[3][0:32, 128:256]]
    pnfv = [psm[2][:, 256:288], psm[3][:, 256:288]]
    pfv = [psm[0][:, 0:32], psm[1][:, 0:32]]

    e_xn = [None] * NSC
    e_xdma = [None] * NSC
    e_s1 = [None] * NSC
    e_sq = [None] * NSC
    e_s2 = [None] * NSC
    e_f = [None] * NSC
    e_fu = [None] * NSC
    e_fmm = [None] * NSC
    e_a = [None] * NSC
    e_l = [None] * NSC
    e_tpev = {}
    e_tplast = [None] * NSC

    for i in range(NSC):
        b = i % NB0
        half = xnT1 if i < 8 else xnT2
        loc = (i % 8) * 128
        war = ([e_xn[i - NB0], e_s1[i - NB0], e_s2[i - NB0]]
               if i >= NB0 else [])
        e_xdma[i] = dma("sync", xbuf[b][:, :], X[i * 128:(i + 1) * 128, :],
                        deps=war, stream=f"x{b}")
        e_s1[i] = vop("reduce_sum", s1t[b][:, :], xbuf[b][:, :], AX.X,
                      deps=[e_xdma[i]])
        e_sq[i] = act(sqbuf[:, :], xbuf[b][:, :], AF.Square,
                      deps=[e_xdma[i]] + ([e_s2[i - 1]] if i >= 1 else []))
        e_s2[i] = vop("reduce_sum", s2t[b][:, :], sqbuf[:, :], AX.X,
                      deps=[e_sq[i]])
        e_m = vop("tensor_scalar", mt[b][:, :], s1t[b][:, :], 1.0 / DM, None,
                  ALU.mult, deps=[e_s1[i]])
        e_m2 = vop("tensor_tensor", m2t[b][:, :], mt[b][:, :], mt[b][:, :],
                   ALU.mult, deps=[e_m])
        e_var = vop("scalar_tensor_tensor", vart[b][:, :], s2t[b][:, :],
                    1.0 / DM, m2t[b][:, :], ALU.mult, ALU.subtract,
                    deps=[e_s2[i], e_m2])
        e_sd = act(sdt[b][:, :], vart[b][:, :], AF.Sqrt, bias=LNEPS, deps=[e_var])
        e_rstd = vop("reciprocal", rstdt[b][:, :], sdt[b][:, :], deps=[e_sd])
        e_nm = vop("scalar_tensor_tensor", nmrs[b][:, :], mt[b][:, :], -1.0,
                   rstdt[b][:, :], ALU.mult, ALU.mult, deps=[e_rstd])
        e_xn[i] = act(xnbuf[b][:, :], xbuf[b][:, :], AF.Identity,
                      bias=nmrs[b][:, 0:1], scale=rstdt[b][:, 0:1],
                      deps=[e_nm, e_xdma[i]]
                      + ([e_tplast[i - NB0], e_fmm[i - NB0]]
                         if i >= NB0 else []))
        for k in range(NK):
            tb = k % 2
            war_t = (e_tpev.get((i, k - 2)) if k >= 2
                     else (e_tpev.get((i - 1, NK - 2 + k)) if i >= 1 else None))
            e_tp = transp(pstv[tb], xnbuf[b][:, k * 128:(k + 1) * 128],
                          identb[:, :], psum=PSM[2 + tb],
                          deps=[e_xn[i], e_idb, war_t])
            e_tpev[(i, k)] = act(half[k][:, loc:loc + 128], pstv[tb], AF.Copy,
                                 deps=[e_tp], psum_r=PSM[2 + tb])
            if k == NK - 1:
                e_tplast[i] = e_tp

        b2 = i % 2
        fv = pfv[b2]
        for k in range(NK):
            d0 = [e_tpev[(i, k)], e_wfb] if k == 0 else [e_tpev[(i, k)]]
            ee = mm(half[k][:, loc:loc + 128], wfb[:, k * 32:(k + 1) * 32], fv,
                    start=(k == 0), stop=(k == NK - 1), psum=PSM[b2],
                    deps=d0, inc=(sPE if k == NK - 1 else None))
        e_fmm[i] = ee
        e_fu[i] = vop("tensor_tensor", ubuf[b2][:, :], fv, bfbc[:, :], ALU.add,
                      deps=[e_fmm[i], e_bfbc] + ([e_a[i - 2]] if i >= 2 else []),
                      psum_r=PSM[b2])
        e_a[i] = act(abuf[b2][:, :], ubuf[b2][:, :], AF.Abs, deps=[e_fu[i]])
        e_e = act(ebuf[b2][:, :], abuf[b2][:, :], AF.Exp, scale=-1.0, deps=[e_a[i]])
        e_l[i] = act(lbuf[b2][:, :], ebuf[b2][:, :], AF.Ln, bias=1.0,
                     deps=[e_e] + ([e_f[i - 2]] if i >= 2 else []))
        e_mn = vop("tensor_scalar", mnbuf[b2][:, :], ubuf[b2][:, :], 0.0, None,
                   ALU.min, deps=[e_fu[i]])
        e_f[i] = vop("tensor_tensor", ftile[i][:, :], mnbuf[b2][:, :],
                     lbuf[b2][:, :], ALU.subtract, deps=[e_mn, e_l[i]])

    e_ftev = [None] * NSC
    for i in range(NSC):
        tb = i % 2
        war_t = e_ftev[i - 2] if i >= 2 else None
        e_tp = transp(pftv[tb], ftile[i][:, :], identf[:, :], psum=PSM[2 + tb],
                      deps=[e_f[i], e_idf, war_t])
        e_ftev[i] = vop("tensor_copy", fT[:, i * 128:(i + 1) * 128], pftv[tb],
                        deps=[e_tp], psum_r=PSM[2 + tb])
    e_scan = vop("tensor_tensor_scan", FcumT[:, :], fT[:, :], fT[:, :], 0.0,
                 ALU.add, ALU.bypass, deps=[e_ftev[NSC - 1]])
    e_fct = dma("sync", FCT, FcumT[:, :], deps=[e_scan], stream="fct")
    e_negF = [None] * NSC
    e_tpnf = None
    for j in range(NSC):
        tb = j % 2
        war_t = e_negF[j - 2] if j >= 2 else None
        e_tpnf = transp(pnfv[tb], FcumT[0:32, j * 128:(j + 1) * 128],
                        identf[0:32, 0:32], psum=PSM[2 + tb],
                        deps=[e_scan, war_t])
        e_negF[j] = vop("tensor_scalar", negF[j][:, :], pnfv[tb], -1.0, None,
                        ALU.mult, deps=[e_tpnf], psum_r=PSM[2 + tb])
    PH0BAR = [e_negF[NSC - 1], e_l[NSC - 1], e_tpnf, e_fct]

    e_tau = [None] * 3
    gstate = {"gi": 0, "red": {}, "prev": None}

    def gamma_steps():
        for w in range(3):
            e_z = vop("memset", gaccr[:, :], 0.0,
                      deps=[gstate["prev"] if w >= 1 else None])
            gstate["prev"] = e_z
            if w < 2:
                G = W1G if w == 0 else W2G
                tiles = [G[cf, hh][:, q2 * 4:(q2 + 1) * 4, :]
                         .rearrange("p k m -> p (k m)")
                         for cf in range(NHF) for hh in range(2)
                         for q2 in range(2)]
            else:
                tiles = [W3G[dq, r][:, cc, :]
                         for dq in range(4) for r in range(22)
                         for cc in range(2)]
            for tile in tiles:
                gi = gstate["gi"]
                b = gi % 2
                e_g = dma("sync", gbuf[b][:, :], tile,
                          deps=[gstate["red"].get(gi - 2)],
                          stream=f"gam{b}")
                e_r = vop("reduce_sum", gpart[b][:, :], gbuf[b][:, :],
                          AX.X, deps=[e_g], apply_absolute_value=True)
                gstate["red"][gi] = e_r
                gstate["prev"] = vop("tensor_tensor", gaccr[:, :], gaccr[:, :],
                                     gpart[b][:, :], ALU.add,
                                     deps=[e_r, gstate["prev"]])
                gstate["gi"] += 1
                yield
            e_s = vop("tensor_reduce", gsum[:, :], gaccr[:, :], AX.C, ALU.add,
                      eng="gpsimd", deps=[gstate["prev"]])
            e_t = vop("tensor_scalar", gsum[:, :], gsum[:, :], 0.5 / (DM * HF),
                      0.5 * QEPS, ALU.mult, ALU.add, eng="gpsimd", deps=[e_s])
            e_w3 = dma("gpsimd", TAU[0:1, w:w + 1], gsum[:, :], deps=[e_t],
                       stream="tau")
            e_tb = dma("gpsimd", taubc[w][:, :],
                       TAU[0:1, w:w + 1].broadcast_to([128, 1]), deps=[e_w3],
                       stream="tau")
            e_tau[w] = vop("tensor_scalar_mul", taubcn[w][:, :], taubc[w][:, :],
                           -1.0, deps=[e_tb])
            gstate["prev"] = e_tau[w]

    _gammagen = gamma_steps()

    def gamma_drain(n):
        for _ in range(n):
            try:
                next(_gammagen)
            except StopIteration:
                return

    gamma_drain(150)

    ph0.close()
    ph1 = contextlib.ExitStack()
    E1 = ph1.enter_context
    wbuf = [E1(nc.sbuf_tensor(f"wbuf{i}", [128, 2048], BF16, side="right")) for i in range(2)]
    QTb = E1(nc.sbuf_tensor("QTb", [128, 2048], BF16, side="right"))
    KTb = E1(nc.sbuf_tensor("KTb", [128, 2048], BF16, side="right"))
    Vb = [E1(nc.sbuf_tensor(f"Vb{s}", [128, 130], BF16, side="right")) for s in range(NSC)]
    bcst = [E1(nc.sbuf_tensor(f"bcst{i}", [128, 2048], F32, side="right"))
            for i in range(2)]
    wstg = [E1(nc.sbuf_tensor(f"wstg{i}", [128, 512], F32, side="right"))
            for i in range(2)]
    ptb = [E1(nc.sbuf_tensor(f"ptb{i}", [128, 512], BF16, side="right")) for i in range(8)]
    OTraw = [E1(nc.sbuf_tensor("OTraw0", [64, 2048], BF16, side="right"))] * 2
    rdrows = E1(nc.sbuf_tensor("rdrows", [33, 2048], F32, side="right"))
    dnrow = rdrows[0:1, :]
    rden = [rdrows[32:33, :], rdrows[32:33, :]]
    denbc = [E1(nc.sbuf_tensor("denbc0", [64, 2048], F32, side="right"))] * 2

    def xnt(k, n):
        half = xnT1 if n < 2 else xnT2
        return half[k][:, (n % 2) * 512:(n % 2) * 512 + 512]

    def xnt128(k, sc):
        half = xnT1 if sc < 8 else xnT2
        loc = (sc % 8) * 128
        return half[k][:, loc:loc + 128]

    e_wl = {}
    wstg_war = {}

    wl_qi = [0]

    def wload(G, c, dstbuf, war_deps):
        evs = None
        for hh in range(4):
            qi = wl_qi[0] % 2
            wl_qi[0] += 1
            wb2 = wstg[qi]
            e_l = dma("scalar", wb2[:, :],
                      G[c][:, hh * 4:(hh + 1) * 4, :].rearrange("p k m -> p (k m)"),
                      deps=[wstg_war.get(qi)]
                      + (war_deps if wstg_war.get(qi) is None else []),
                      stream=f"ws{qi}")
            evs = vop("tensor_copy", dstbuf[:, hh * 512:(hh + 1) * 512],
                      wb2[:, :], deps=[e_l] + (war_deps if hh == 0 else []))
            wstg_war[qi] = evs
        return evs

    e_qk_done = {}
    e_vlast = {}
    e_pair_pv = {}
    e_head_done = {}
    e_bc = {}
    war_psO = {}
    stv_last = {}
    war_st = {}
    war_pt = {}
    e_rd_pr = {}

    for c in range(NPAIR):
        gamma_drain(26)
        wb_q = wbuf[c % 2]
        wb_k = wbuf[(c + 1) % 2]
        e_wl[(c, "q")] = wload(WQG, c, wb_q,
                               [e_qk_done.get(c - 1), e_wfb]
                               + (PH0BAR if c == 0 else []))
        e_wl[(c, "k")] = wload(WKG, c, wb_k,
                               [e_vlast.get(c - 1)]
                               + (PH0BAR if c == 0 else []))
        e_qmm = [None] * 4
        for k in range(NK):
            for n in range(4):
                d0 = [e_wl[(c, "q")]] if k == 0 else []
                e_qmm[n] = mm(wb_q[:, k * 128:(k + 1) * 128], xnt(k, n),
                              psm[n][:, :], start=(k == 0), stop=(k == NK - 1),
                              psum=PSM[n], deps=d0,
                              inc=(sPE if k == NK - 1 else None))
        e_qev = [act(QTb[:, n * 512:(n + 1) * 512], psm[n][:, :], AF.Identity,
                     scale=0.125,
                     deps=[e_qmm[n], e_pair_pv.get(c - 1)],
                     psum_r=PSM[n]) for n in range(4)]
        e_kmm = [None] * 4
        for k in range(NK):
            for n in range(4):
                d0 = [e_wl[(c, "k")]] if k == 0 else []
                e_kmm[n] = mm(wb_k[:, k * 128:(k + 1) * 128], xnt(k, n),
                              psm[n][:, :], start=(k == 0), stop=(k == NK - 1),
                              psum=PSM[n], deps=d0,
                              inc=(sPE if k == NK - 1 else None))
        e_kev = [act(KTb[:, n * 512:(n + 1) * 512], psm[n][:, :], AF.Copy,
                     deps=[e_kmm[n], e_pair_pv.get(c - 1)],
                     psum_r=PSM[n]) for n in range(4)]
        e_qk_done[c] = e_kmm[3]
        e_wl[(c, "v")] = wload(WVG, c, wb_q, [e_qmm[3]])
        psvv = [psm[2][:, 0:128], psm[3][:, 0:128]]
        e_vev = [None] * NSC
        for sc in range(NSC):
            tb = sc % 2
            e_vmm = None
            for k in range(NK):
                d0 = [e_wl[(c, "v")]] if k == 0 else []
                e_vmm = mm(xnt128(k, sc), wb_q[:, k * 128:(k + 1) * 128],
                           psvv[tb], start=(k == 0), stop=(k == NK - 1),
                           psum=PSM[2 + tb], deps=d0,
                           inc=(sPE if k == NK - 1 else None))
            e_ms = vop("memset", Vb[sc][:, :], 1.0,
                       deps=[e_pair_pv.get(c - 1)]
                       + (PH0BAR if c == 0 else []))
            vop("tensor_copy", Vb[sc][:, 0:64], psvv[tb][:, 0:64],
                deps=[e_vmm, e_ms])
            e_vev[sc] = vop("tensor_copy", Vb[sc][:, 65:129],
                            psvv[tb][:, 64:128],
                            deps=[e_vmm, e_ms], psum_r=PSM[2 + tb])
            if sc == NSC - 1:
                e_vlast[c] = e_vmm
        for hl in range(2):
            h = 2 * c + hl
            hb = h % 2
            e_bc[h] = dma("scalar", bcst[hb][:, :],
                          FCT[h:h + 1, :].broadcast_to([128, 2048]),
                          deps=[e_fct, stv_last.get(h - 2)]
                          + (PH0BAR if h <= 1 else []),
                          stream=f"bc{hb}")
            e_pv_last = None
            tiles = []
            for j in range(NSC):
                for bb in range(j // 4, 4):
                    cs = max(j * 128, bb * 512)
                    tiles.append((j, bb, cs, (bb + 1) * 512 - cs))
            GG = 4
            ngr = (len(tiles) + GG - 1) // GG
            tev = {}

            def emit_smm_group(g):
                for t in range(g * GG, min((g + 1) * GG, len(tiles))):
                    j, bb, cs, wdt = tiles[t]
                    sb = t % 4
                    p8 = t % 8
                    off = cs - bb * 512
                    pv_sc = psm[sb][:, off:off + wdt]
                    d0 = [e_kev[3]] if t == 0 else []
                    e_smm = mm(KTb[hl * 64:hl * 64 + 64,
                                   j * 128:(j + 1) * 128],
                               QTb[hl * 64:hl * 64 + 64, cs:cs + wdt],
                               pv_sc, start=True, stop=True, psum=PSM[sb],
                               deps=d0, inc=sPE)
                    e_stv = vop("scalar_tensor_tensor", pv_sc, pv_sc, 1.0,
                                bcst[hb][:, cs:cs + wdt], ALU.mult, ALU.add,
                                deps=[e_smm, e_bc[h]])
                    if cs == j * 128:
                        e_stv = vop("scalar_tensor_tensor",
                                    psm[sb][:, off:off + 128],
                                    psm[sb][:, off:off + 128], 1.0,
                                    negm[:, :], ALU.mult, ALU.add,
                                    deps=[e_stv, e_negm])
                    stv_last[h] = e_stv
                    e_p = act(ptb[p8][:, 0:wdt], pv_sc, AF.Exp,
                              bias=negF[j][:, h:h + 1],
                              deps=[e_stv, e_negF[j], war_pt.get(p8)],
                              psum_r=PSM[sb])
                    tev[t] = e_p

            def emit_pv_group(g):
                nonlocal e_pv_last
                for t in range(g * GG, min((g + 1) * GG, len(tiles))):
                    j, bb, cs, wdt = tiles[t]
                    p8 = t % 8
                    e_pv_last = mm(Vb[j][:, hl * 65:hl * 65 + 65],
                                   ptb[p8][:, 0:wdt],
                                   psO[0:65, cs:cs + wdt], start=(j == 0),
                                   stop=(j == min(15, 4 * bb + 3)), psum="psO",
                                   deps=[tev[t], e_vev[j]]
                                   + ([war_psO.get(0)] if t == 0 else []),
                                   inc=sPE)
                    war_pt[p8] = e_pv_last

            emit_smm_group(0)
            for g in range(1, ngr):
                emit_smm_group(g)
                emit_pv_group(g - 1)
            emit_pv_group(ngr - 1)
            e_pair_pv[c] = e_pv_last
            e_otr = act(OTraw[hb][:, :], psO[0:64, :], AF.Copy,
                        deps=[e_pv_last, e_head_done.get(h - 1)])
            war_psO_act = e_otr
            e_rd = vop("reciprocal", rden[hb], psO[64:65, :],
                       deps=[e_pv_last, e_rd_pr.get(0)], psum_r="psO")
            e_rw = dma("scalar", RDN[h:h + 1, :], rden[hb], deps=[e_rd], stream="rdn")
            e_rd_pr[0] = e_rw
            e_rb = dma("gpsimd", denbc[hb][:, :],
                       RDN[h:h + 1, :].broadcast_to([64, 2048]),
                       deps=[e_rw, e_head_done.get(h - 1)], stream="db")
            war_psO[0] = war_psO_act
            e_head_done[h] = vop("tensor_tensor",
                                 OT[c][hl * 64:hl * 64 + 64, :],
                                 OTraw[hb][:, :], denbc[hb][:, :], ALU.mult,
                                 deps=[e_otr, e_rb])

    PH1BAR = [e_head_done[H - 1], e_rd_pr[0], e_pair_pv[NPAIR - 1]]

    ph1.close()
    ph2 = contextlib.ExitStack()
    E2 = ph2.enter_context
    stgq = [E2(nc.sbuf_tensor(f"stgq{i}", [128, 1024], F32, side="right"))
            for i in range(2)]
    qoutb = [E2(nc.sbuf_tensor(f"qoutb{i}", [128, 1024], BF16, side="right"))
             for i in range(2)]
    gtb2 = [E2(nc.sbuf_tensor(f"gtb2{i}", [128, 1024], BF16, side="right"))
            for i in range(2)]
    ltb2 = [E2(nc.sbuf_tensor(f"ltb2{i}", [128, 1024], BF16, side="right"))
            for i in range(2)]

    e_qst = {}
    qp_state = {"qi": 0, "sub": {}, "store": {}, "lt": {}}

    def qpass_half(srcap, dstap, w, key, pre):
        qi = qp_state["qi"]
        qp_state["qi"] += 1
        buf = stgq[qi % 2]
        e_l = dma("sync", buf[:, :], srcap,
                  deps=pre + [qp_state["sub"].get(qi - 2), e_tau[w]],
                  stream=f"qp{qi % 2}")
        e_gt = vop("tensor_single_scalar", gtb2[qi % 2][:, :], buf[:, :],
                   taubc[w][:, 0:1], ALU.is_gt,
                   deps=[e_l, qp_state["sub"].get(qi - 2)])
        e_lt = vop("tensor_single_scalar", ltb2[qi % 2][:, :], buf[:, :],
                   taubcn[w][:, 0:1], ALU.is_lt,
                   deps=[e_l, qp_state["sub"].get(qi - 2)])
        e_sub = vop("tensor_tensor", qoutb[qi % 2][:, :], gtb2[qi % 2][:, :],
                    ltb2[qi % 2][:, :], ALU.subtract,
                    deps=[e_gt, e_lt, qp_state["store"].get(qi - 2)],
                    eng="gpsimd")
        qp_state["sub"][qi] = e_sub
        ev = dma("scalar", dstap, qoutb[qi % 2][:, :], deps=[e_sub],
                 stream=f"qs{qi % 2}")
        qp_state["store"][qi] = ev
        e_qst[key] = ev

    gamma_drain(1000)
    first = PH1BAR
    for cf in range(NHF):
        for w, G, D in ((0, W1G, Q1D), (1, W2G, Q2D)):
            for hh in range(2):
                qpass_half(G[cf, hh].rearrange("p k m -> p (k m)"),
                           D[cf][:, hh * 1024:hh * 1024 + 1024], w,
                           (w, cf), first)
                first = []
    for dq in range(4):
        for r4 in range(11):
            for hh in range(2):
                qpass_half(W3G[dq, r4 * 2 + hh].rearrange("p cc d -> p (cc d)"),
                           Q3D[dq, r4][:, hh * 1024:hh * 1024 + 1024], 2,
                           (2, dq, r4), [])

    wob = [E2(nc.sbuf_tensor(f"wob{c}", [128, 1024], BF16, side="right")) for c in range(NK)]
    wostg = [E2(nc.sbuf_tensor(f"wostg{i}", [128, 1024], F32, side="right"))
             for i in range(2)]
    attnb = [E2(nc.sbuf_tensor(f"attnb{i}", [128, 512], F32, side="right")) for i in range(4)]
    attn_war = {}
    e_dh_last = None
    e_attn_last = {}
    e_wo_pr = {}
    for dh in range(2):
        e_wo = []
        for cc in range(NK):
            wb2 = wostg[cc % 2]
            e_l = dma("scalar", wb2[:, :],
                      WOT[cc * 128:(cc + 1) * 128, dh * 1024:(dh + 1) * 1024],
                      deps=([e_dh_last] if dh == 1 and cc < 2 else [])
                      + [e_wo_pr.get(cc % 2)]
                      + (PH1BAR if dh == 0 and cc < 2 else []),
                      stream=f"ws{cc % 2}")
            ev = act(wob[cc][:, :], wb2[:, :], AF.Copy,
                     deps=[e_l] + ([e_dh_last] if dh == 1 else []))
            e_wo_pr[cc % 2] = ev
            e_wo.append(ev)
        for si in range(NSC):
            for bbo in range(2):
                pp = (si * 2 + bbo) % 4
                e_mmf = None
                for cc in range(NK):
                    d0 = [e_wo[cc]]
                    if cc == 0:
                        d0.append(e_head_done[H - 1])
                    e_mmf = mm(OT[cc][:, si * 128:(si + 1) * 128],
                               wob[cc][:, bbo * 512:(bbo + 1) * 512],
                               psm[pp][:, :], start=(cc == 0),
                               stop=(cc == NK - 1), psum=PSM[pp], deps=d0,
                               inc=(sPE if cc == NK - 1 else None))
                e_ev = act(attnb[pp][:, :], psm[pp][:, :], AF.Copy,
                           deps=[e_mmf, attn_war.get(pp)], psum_r=PSM[pp])
                e_attn_last[pp] = dma(
                    "sync",
                    ATTN[si * 128:(si + 1) * 128,
                         dh * 1024 + bbo * 512:dh * 1024 + bbo * 512 + 512],
                    attnb[pp][:, :], deps=[e_ev], stream=f"at{pp}")
                attn_war[pp] = e_attn_last[pp]
        e_dh_last = e_mmf
    e_ph2_done = [e_attn_last[pp] for pp in range(4)]

    ph2.close()
    phot.close()
    phg.close()
    ph3 = contextlib.ExitStack()
    E3 = ph3.enter_context
    hid = [E3(nc.sbuf_tensor(f"hid{cf}", [128, 1024], BF16, side="right")) for cf in range(NHF)]

    q1b = [E3(nc.sbuf_tensor(f"q1b{i}", [128, 2048], BF16, side="right"))
           for i in range(2)]
    q2b = [E3(nc.sbuf_tensor(f"q2b{i}", [128, 2048], BF16, side="right"))
           for i in range(2)]
    sgb = E3(nc.sbuf_tensor("sgb", [128, 512], F32, side="right"))


    e_w3ph = [None, None]
    e_hid = {}
    war_x = {}
    war_a = {}
    war_o = {}
    st3 = {"e_q3_pr": None, "last_y": None, "w3mm": None}

    def emit_x1x2(sh):
        xnTh = xnT1 if sh == 0 else xnT2
        e_q1 = e_q2 = None
        e_hev_pr = {}
        e_xmm_pr = {}
        e_xmm_by_cf = {}
        for cf in range(NHF):
            qp = cf % 2
            d_pre = (e_ph2_done if sh == 0 and cf == 0 else []) + [
                     e_w3ph[0] if sh == 1 and cf == 0 else None,
                     e_xmm_by_cf.get(cf - 2)]
            e_q1 = dma("sync", q1b[qp][:, :], Q1D[cf],
                       deps=d_pre + [e_qst[(0, cf)], st3.get("lq1")],
                       stream="w12a")
            st3["lq1"] = e_q1
            e_q2 = dma("sync", q2b[qp][:, :], Q2D[cf],
                       deps=[e_qst[(1, cf)], e_xmm_by_cf.get(cf - 2),
                             st3.get("lq2")], stream="w12b")
            st3["lq2"] = e_q2
            e_x1 = [None, None]
            e_x2 = [None, None]
            for k in range(NK):
                for n in range(4):
                    qq = q1b[qp] if n < 2 else q2b[qp]
                    eq = e_q1 if n < 2 else e_q2
                    d0 = []
                    if k == 0:
                        d0 = [eq]
                        if sh == 1 and cf == 0:
                            d0.append(e_w3ph[0])
                    ee = mm(qq[:, k * 128:(k + 1) * 128],
                            xnTh[k][:, (n % 2) * 512:(n % 2) * 512 + 512],
                            psm[n][:, :], start=(k == 0), stop=(k == NK - 1),
                            psum=PSM[n], deps=d0,
                            inc=(sPE if k == NK - 1 else None))
                    if n < 2:
                        e_x1[n] = ee
                    else:
                        e_x2[n - 2] = ee
                    if k == NK - 1:
                        e_xmm_pr[n] = ee
                        if n == 3:
                            e_xmm_by_cf[cf] = ee
            for n in range(2):
                e_sgm = act(sgb[:, 0:512], psm[n][:, :],
                            AF.Sigmoid,
                            deps=[e_x1[n],
                                  e_hev_pr.get((cf, 2) if n == 1
                                               else (cf - 1, 3))],
                            psum_r=PSM[n])
                e_hev_pr[(cf, n)] = e_sgm
                ph = vop("tensor_tensor", hid[cf][:, n * 512:(n + 1) * 512],
                         sgb[:, 0:512], psm[2 + n][:, :],
                         ALU.mult,
                         deps=[e_sgm, e_x2[n],
                               e_w3ph[0] if sh == 1 else None],
                         psum_r=PSM[2 + n])
                e_hev_pr[(cf, 2 + n)] = ph
                e_hid[(sh, cf)] = ph

    emit_x1x2(0)

    phx1.close()
    phnf.close()
    q3t = [E3(nc.sbuf_tensor(f"q3t{r}", [128, 2048], BF16)) for r in range(11)]
    xres = [E3(nc.sbuf_tensor(f"xres{i}", [128, 512], F32)) for i in range(2)]
    ares = [E3(nc.sbuf_tensor(f"ares{i}", [128, 512], F32)) for i in range(2)]

    def emit_w3(sh):
        for dq in range(4):
            e_q3cf = [None] * NHF
            for r4 in range(11):
                ev = dma("sync", q3t[r4][:, :], Q3D[dq, r4],
                         deps=[st3["e_q3_pr"] if r4 == 0 else None,
                               st3["w3mm"], e_qst[(2, dq, r4)],
                               st3.get("lq3"),
                               e_hid[(sh, NHF - 1)] if dq == 0 and r4 == 0
                               else None], stream="w12c")
                st3["lq3"] = ev
                for cf in range(r4 * 4, r4 * 4 + 4):
                    e_q3cf[cf] = ev
            st3["e_q3_pr"] = e_q3cf[NHF - 1]
            for sp in range(4):
                sweep0 = (sp == 0)
                pps = [(sp * 2) % 4, (sp * 2 + 1) % 4]
                e_mmf = [None, None]
                for cf in range(NHF):
                    for z in range(2):
                        si = sp * 2 + z
                        d0 = [e_q3cf[cf]] if sweep0 else []
                        if cf == 0:
                            d0 = d0 + [e_hid[(sh, NHF - 1)]]
                        e_mmf[z] = mm(hid[cf][:, si * 128:(si + 1) * 128],
                                      q3t[cf // 4][:, (cf % 4) * 512:
                                                   (cf % 4) * 512 + 512],
                                      psm[pps[z]][:, :], start=(cf == 0),
                                      stop=(cf == NHF - 1), psum=PSM[pps[z]],
                                      deps=d0,
                                      inc=(sPE if cf == NHF - 1 else None))
                st3["w3mm"] = e_mmf[1]
                for z in range(2):
                    si = sp * 2 + z
                    sig = sh * 8 + si
                    e_xr = dma("sync", xres[z][:, :],
                               X[sig * 128:(sig + 1) * 128,
                                 dq * 512:(dq + 1) * 512],
                               deps=[war_x.get(z),
                                     e_hid[(sh, NHF - 1)] if dq == 0
                                     and sp == 0 else None],
                               stream=f"x{z}")
                    e_ar = dma("gpsimd", ares[z][:, :],
                               ATTN[sig * 128:(sig + 1) * 128,
                                    dq * 512:(dq + 1) * 512],
                               deps=[war_a.get(z),
                                     e_hid[(sh, NHF - 1)] if dq == 0
                                     and sp == 0 else None]
                               + (e_ph2_done if sh == 0 and dq == 0
                                  and sp == 0 else []),
                               stream=f"ar{z}")
                    e_t1 = vop("tensor_tensor", xres[z][:, :], xres[z][:, :],
                               psm[pps[z]][:, :], ALU.add,
                               deps=[e_mmf[z], e_xr],
                               psum_r=PSM[pps[z]])
                    e_t2 = vop("tensor_tensor", xres[z][:, :], xres[z][:, :],
                               ares[z][:, :], ALU.add,
                               deps=[e_t1, e_ar])
                    war_a[z] = e_t2
                    st3["last_y"] = dma(
                        "scalar",
                        Y[sig * 128:(sig + 1) * 128, dq * 512:(dq + 1) * 512],
                        xres[z][:, :], deps=[e_t2], stream=f"y{z}")
                    war_x[z] = st3["last_y"]
        e_w3ph[sh] = st3["last_y"]

    emit_w3(0)
    emit_x1x2(1)
    emit_w3(1)

    n_out_per = 2 * 4 * 4
    P.op("sync", None, deps=[(stream_sem("y0"), 16 * n_out_per),
                             (stream_sem("y1"), 16 * n_out_per)])

    P.emit(nc, preamble=e_constv)
    ph3.close()
    st.close()
    return nc


_NC_CACHE = None


def kernel(L0_attn_hidden_states, L0_attn_Wq, L0_attn_Wk, L0_attn_Wv,
           L0_attn_Wf, L0_attn_bf, L0_attn_Wo, L0_ffn_w1, L0_ffn_w2, L0_ffn_w3):
    global _NC_CACHE
    hs = np.asarray(L0_attn_hidden_states)
    B = hs.shape[0]
    assert hs.shape == (8, S, DM)

    def colgroup_p(wt):
        return np.ascontiguousarray(
            wt.reshape(16, 128, 16, 128).transpose(2, 1, 0, 3))

    def w12group(wt):
        return np.ascontiguousarray(
            wt.reshape(2, 8, 128, 44, 128).transpose(3, 0, 2, 1, 4))

    def w3group(wt):
        return np.ascontiguousarray(
            wt.reshape(22, 2, 128, 4, 512).transpose(3, 0, 2, 1, 4))

    wqT = np.ascontiguousarray(np.asarray(L0_attn_Wq).T)
    wkT = np.ascontiguousarray(np.asarray(L0_attn_Wk).T)
    wvT = np.ascontiguousarray(np.asarray(L0_attn_Wv).T)
    wfT = np.ascontiguousarray(np.asarray(L0_attn_Wf).T)
    woT = np.ascontiguousarray(np.asarray(L0_attn_Wo).T)
    w1T = np.ascontiguousarray(np.asarray(L0_ffn_w1).T)
    w2T = np.ascontiguousarray(np.asarray(L0_ffn_w2).T)
    w3T = np.ascontiguousarray(np.asarray(L0_ffn_w3).T)
    common = {
        "wqg": colgroup_p(wqT),
        "wkg": colgroup_p(wkT),
        "wvg": colgroup_p(wvT),
        "wft": wfT,
        "bfv": np.asarray(L0_attn_bf).reshape(1, H).astype(np.float32),
        "wot": woT,
        "w1g": w12group(w1T),
        "w2g": w12group(w2T),
        "w3g": w3group(w3T),
        "idb": np.eye(128, dtype=bfdt),
        "idf": np.eye(128, dtype=np.float32),
        "triu": np.triu(np.ones((128, 128), dtype=np.float32)).astype(bfdt),
        "negm": ((1.0 - np.triu(np.ones((128, 128), dtype=np.float32)))
                 * -1e9).astype(np.float32),
        "constv": np.tile(np.array([CONST_VALS], dtype=np.float32), (128, 1)),
    }
    in_maps = [dict(common, x=np.ascontiguousarray(hs[c])) for c in range(B)]
    if _NC_CACHE is None:
        _NC_CACHE = build_nc()
    res = run_bass_kernel_spmd(_NC_CACHE, in_maps, core_ids=list(range(B)))
    return np.stack([res.results[c]["y"] for c in range(B)], axis=0)


# version 13
# speedup vs baseline: 1.0954x; 1.0954x over previous
import contextlib
import numpy as np
import ml_dtypes
import concourse.bass as bass
import concourse.mybir as mybir
from concourse.bass_utils import run_bass_kernel_spmd

F32 = mybir.dt.float32
BF16 = mybir.dt.bfloat16
AF = mybir.ActivationFunctionType
ALU = mybir.AluOpType
AX = mybir.AxisListType
bfdt = ml_dtypes.bfloat16

S = 2048
DM = 2048
H = 32
HF = 5632
NK = DM // 128
NSC = S // 128
NPAIR = H // 2
NHF = HF // 128
QEPS = 1e-8
LNEPS = 1e-5

ENGINES = ("sync", "gpsimd", "scalar", "vector", "tensor")
CONST_VALS = [0.0, 1.0, -1.0, LNEPS, 0.125, 0.5, 2.0, 3.0]


class Prog:

    def __init__(self):
        self.ops = {e: [] for e in ENGINES}
        self.cnt = {}
        self.psum_dirty = {}
        self.pe_sat = None
        self.last_reader = {}

    def _bump(self, sem, by):
        c = self.cnt.get(sem.name, 0) + by
        self.cnt[sem.name] = c
        return (sem, c)

    def op(self, eng, emit, deps=(), inc=None, inc_by=1, psum_w=None, psum_r=None):
        deps = [d for d in deps if d is not None]
        if eng == "tensor" and psum_w is not None:
            lr = self.last_reader.pop(psum_w, None)
            if lr is not None:
                deps.append(lr)
            if self.psum_dirty.get(psum_w) and not deps:
                assert self.pe_sat is not None, "matmul separator unavailable"
                deps.append(self.pe_sat)
        if eng == "tensor" and deps:
            for k in self.psum_dirty:
                self.psum_dirty[k] = False
            self.pe_sat = deps[-1]
        if eng == "tensor" and psum_w is not None:
            self.psum_dirty[psum_w] = True
        dd = {}
        for (sem, val) in deps:
            if val > dd.get(sem.name, (None, -1))[1]:
                dd[sem.name] = (sem, val)
        deps = tuple(dd.values())
        ev = None
        if inc is not None:
            ev = self._bump(inc, inc_by)
        if psum_r is not None:
            assert ev is not None
            for nm in (psum_r if isinstance(psum_r, (list, tuple)) else [psum_r]):
                self.last_reader[nm] = ev
        self.ops[eng].append((emit, deps, ((inc, inc_by),) if inc is not None else ()))
        return ev

    def emit(self, nc, preamble=None):
        with nc.Block() as blk:
            for eng_name in ENGINES:
                oplist = self.ops[eng_name]
                if preamble is not None and eng_name != "sync":
                    oplist = [(None, (preamble,), ())] + oplist

                def make(oplist):
                    def fn(eng):
                        for (emit, deps, incs) in oplist:
                            for (sem, val) in deps:
                                eng.wait_ge(sem, val)
                            if emit is None:
                                continue
                            inst = emit(eng)
                            for (sem, by) in incs:
                                inst.then_inc(sem, by)
                    return fn

                getattr(blk, eng_name)(make(oplist))


def build_nc(debug=False):
    nc = bass.Bass("TRN2", target_bir_lowering=False,
                   dynamic_dma_scratch_size=16384)

    X = nc.dram_tensor("x", [S, DM], F32, kind="ExternalInput").ap()
    WQG = nc.dram_tensor("wqg", [NPAIR, 128, NK, 128], F32, kind="ExternalInput").ap()
    WKG = nc.dram_tensor("wkg", [NPAIR, 128, NK, 128], F32, kind="ExternalInput").ap()
    WVG = nc.dram_tensor("wvg", [NPAIR, 128, NK, 128], F32, kind="ExternalInput").ap()
    WFT = nc.dram_tensor("wft", [DM, H], F32, kind="ExternalInput").ap()
    BF = nc.dram_tensor("bfv", [1, H], F32, kind="ExternalInput").ap()
    WOT = nc.dram_tensor("wot", [DM, DM], F32, kind="ExternalInput").ap()
    W1G = nc.dram_tensor("w1g", [NHF, 2, 128, 8, 128], F32, kind="ExternalInput").ap()
    W2G = nc.dram_tensor("w2g", [NHF, 2, 128, 8, 128], F32, kind="ExternalInput").ap()
    W3G = nc.dram_tensor("w3g", [4, 22, 128, 2, 512], F32, kind="ExternalInput").ap()
    IDB = nc.dram_tensor("idb", [128, 128], BF16, kind="ExternalInput").ap()
    IDF = nc.dram_tensor("idf", [128, 128], F32, kind="ExternalInput").ap()
    NEGM = nc.dram_tensor("negm", [128, 128], F32, kind="ExternalInput").ap()
    CONSTV = nc.dram_tensor("constv", [128, 8], F32, kind="ExternalInput").ap()
    Y = nc.dram_tensor("y", [S, DM], F32, kind="ExternalOutput").ap()
    dbg = dict(kind="ExternalOutput") if debug else {}
    ATTN = nc.dram_tensor("attn_scr", [S, DM], F32, **dbg).ap()
    FCT = nc.dram_tensor("fct_scr", [H, S], F32, **dbg).ap()
    RDN = nc.dram_tensor("rdn_scr", [H, S], F32, **dbg).ap()
    GP = nc.dram_tensor("gp_scr", [3, 128], F32).ap()
    TAU = nc.dram_tensor("tau_scr", [1, 4], F32).ap()
    Q1D = nc.dram_tensor("q1_scr", [NHF, 128, 2048], BF16).ap()
    Q2D = nc.dram_tensor("q2_scr", [NHF, 128, 2048], BF16).ap()
    Q3D = nc.dram_tensor("q3_scr", [4, 11, 128, 2048], BF16).ap()

    P = Prog()
    st = contextlib.ExitStack()
    E = st.enter_context

    sDs = E(nc.semaphore("sDs"))
    sDg = E(nc.semaphore("sDg"))
    sDc = E(nc.semaphore("sDc"))
    sPE = E(nc.semaphore("sPE"))
    sA = E(nc.semaphore("sA"))
    sV = E(nc.semaphore("sV"))
    sG = E(nc.semaphore("sG"))
    sOUT = E(nc.semaphore("sOUT"))

    SEMS = {}

    def stream_sem(name):
        if name not in SEMS:
            SEMS[name] = E(nc.semaphore("sdma_" + name))
        return SEMS[name]

    def dma(q, out, in_, deps=(), stream=None, slow=False):
        assert stream is not None
        s = stream_sem(stream)

        def em(e, out=out, in_=in_, slow=slow):
            if slow:
                return e.dma_start(out=out, in_=in_, allow_slow_non_contiguous=True)
            return e.dma_start(out=out, in_=in_)
        return P.op(q, em, deps=deps, inc=s, inc_by=16)

    def mm(lhsT, rhs, out, start, stop, psum, deps=(), inc=None):
        def em(e, out=out, lhsT=lhsT, rhs=rhs, start=start, stop=stop):
            return e.matmul(out, lhsT, rhs, start=start, stop=stop)
        return P.op("tensor", em, deps=deps, inc=inc, psum_w=psum)

    def transp(out, in_, ident, psum, deps=(), inc=sPE):
        def em(e, out=out, in_=in_, ident=ident):
            return e.transpose(out, in_, ident)
        return P.op("tensor", em, deps=deps, inc=inc, psum_w=psum)

    def act(out, in_, func, bias=0.0, scale=1.0, deps=(), psum_r=None):
        def em(e, out=out, in_=in_, func=func, bias=bias, scale=scale):
            return e.activation(out=out, in_=in_, func=func, bias=bias, scale=scale)
        return P.op("scalar", em, deps=deps, inc=sA, psum_r=psum_r)

    def vop(name, *args, deps=(), psum_r=None, eng="vector", **kw):
        def em(e, name=name, args=args, kw=kw):
            return getattr(e, name)(*args, **kw)
        return P.op(eng, em, deps=deps, inc=(sG if eng == "gpsimd" else sV),
                    psum_r=psum_r)

    xnT1 = None
    xnT2 = [E(nc.sbuf_tensor(f"xnT2_{k}", [128, 1024], BF16)) for k in range(NK)]
    identb = E(nc.sbuf_tensor("identb", [128, 128], BF16))
    negm = E(nc.sbuf_tensor("negm_sb", [128, 128], F32))
    taubc = [E(nc.sbuf_tensor(f"taubc{w}", [128, 1], F32)) for w in range(3)]
    taubcn = [E(nc.sbuf_tensor(f"taubcn{w}", [128, 1], F32)) for w in range(3)]
    constv = E(nc.sbuf_tensor("constv_sb", [128, 8], F32))
    phg = contextlib.ExitStack()
    Eg = phg.enter_context
    gbuf = [Eg(nc.sbuf_tensor(f"gbuf{i}", [128, 1024], F32, side="right")) for i in range(2)]
    gpart = [Eg(nc.sbuf_tensor(f"gpart{i}", [128, 1], F32, side="right"))
             for i in range(2)]
    gaccr = Eg(nc.sbuf_tensor("gaccr", [128, 1], F32, side="right"))
    gsum = Eg(nc.sbuf_tensor("gsum", [1, 1], F32, side="right"))
    phnf = contextlib.ExitStack()
    negF = [phnf.enter_context(nc.sbuf_tensor(f"negF{j}", [128, 32], F32))
            for j in range(NSC)]
    phot = contextlib.ExitStack()
    OT = [phot.enter_context(nc.sbuf_tensor(f"OT{c}", [128, 2048], BF16, side="right"))
          for c in range(NPAIR)]
    phx1 = contextlib.ExitStack()

    psm = [E(nc.psum_tensor(f"psm{i}", [128, 512], F32)) for i in range(4)]
    psO = E(nc.psum_tensor("psO", [65, 2048], F32))
    PSM = [f"psm{i}" for i in range(4)]

    xnT1 = [phx1.enter_context(nc.sbuf_tensor(f"xnT1_{k}", [128, 1024], BF16))
            for k in range(NK)]

    e_constv = dma("sync", constv[:, :], CONSTV, stream="const")
    for i, v in enumerate(CONST_VALS):
        try:
            nc.const_aps.register(F32, v, constv[:, i:i + 1])
        except Exception:
            pass
    dma("sync", identb[:, :], IDB, stream="const")
    dma("sync", negm[:, :], NEGM, stream="const")

    ph0 = contextlib.ExitStack()
    E0 = ph0.enter_context
    NB0 = 3
    identf = E0(nc.sbuf_tensor("identf", [128, 128], F32, side="right"))
    xbuf = [E0(nc.sbuf_tensor(f"xbuf{i}", [128, 2048], F32, side="right")) for i in range(NB0)]
    sqbuf = E0(nc.sbuf_tensor("sqbuf", [128, 2048], F32, side="right"))
    xnbuf = [E0(nc.sbuf_tensor(f"xnbuf{i}", [128, 2048], BF16, side="right")) for i in range(NB0)]
    s1t = [E0(nc.sbuf_tensor(f"s1t{i}", [128, 1], F32, side="right")) for i in range(NB0)]
    s2t = [E0(nc.sbuf_tensor(f"s2t{i}", [128, 1], F32, side="right")) for i in range(NB0)]
    mt = [E0(nc.sbuf_tensor(f"mt{i}", [128, 1], F32, side="right")) for i in range(NB0)]
    m2t = [E0(nc.sbuf_tensor(f"m2t{i}", [128, 1], F32, side="right")) for i in range(NB0)]
    vart = [E0(nc.sbuf_tensor(f"vart{i}", [128, 1], F32, side="right")) for i in range(NB0)]
    sdt = [E0(nc.sbuf_tensor(f"sdt{i}", [128, 1], F32, side="right")) for i in range(NB0)]
    rstdt = [E0(nc.sbuf_tensor(f"rstdt{i}", [128, 1], F32, side="right")) for i in range(NB0)]
    nmrs = [E0(nc.sbuf_tensor(f"nmrs{i}", [128, 1], F32, side="right")) for i in range(NB0)]
    ftile = [E0(nc.sbuf_tensor(f"ftile{i}", [128, 32], F32, side="right")) for i in range(NSC)]
    fT = E0(nc.sbuf_tensor("fT", [32, 2048], F32, side="right"))
    FcumT = E0(nc.sbuf_tensor("FcumT", [32, 2048], F32, side="right"))
    ubuf = [E0(nc.sbuf_tensor(f"ubuf{i}", [128, 32], F32, side="right")) for i in range(2)]
    abuf = [E0(nc.sbuf_tensor(f"abuf{i}", [128, 32], F32, side="right")) for i in range(2)]
    ebuf = [E0(nc.sbuf_tensor(f"ebuf{i}", [128, 32], F32, side="right")) for i in range(2)]
    lbuf = [E0(nc.sbuf_tensor(f"lbuf{i}", [128, 32], F32, side="right")) for i in range(2)]
    mnbuf = [E0(nc.sbuf_tensor(f"mnbuf{i}", [128, 32], F32, side="right")) for i in range(2)]
    wfb = E0(nc.sbuf_tensor("wfb", [128, 512], BF16, side="right"))
    bfbc = E0(nc.sbuf_tensor("bfbc", [128, 32], F32, side="right"))

    dma("sync", identf[:, :], IDF, stream="const")
    e_wfb = dma("gpsimd", wfb[:, :].rearrange("p (k h) -> p k h", k=NK),
                WFT.rearrange("(k p) h -> p k h", p=128), stream="wfb")
    E_CONST = dma("sync", bfbc[:, :], BF.broadcast_to([128, H]), stream="const")
    e_bfbc = E_CONST
    e_idb = e_idf = e_triu = e_negm = E_CONST

    pstv = [psm[2][:, 0:64].bitcast(BF16), psm[3][:, 0:64].bitcast(BF16)]
    pftv = [psm[2][0:32, 128:256], psm[3][0:32, 128:256]]
    pnfv = [psm[2][:, 256:288], psm[3][:, 256:288]]
    pfv = [psm[0][:, 0:32], psm[1][:, 0:32]]

    e_xn = [None] * NSC
    e_xdma = [None] * NSC
    e_s1 = [None] * NSC
    e_sq = [None] * NSC
    e_s2 = [None] * NSC
    e_f = [None] * NSC
    e_fu = [None] * NSC
    e_fmm = [None] * NSC
    e_a = [None] * NSC
    e_l = [None] * NSC
    e_tpev = {}
    e_tplast = [None] * NSC

    for i in range(NSC):
        b = i % NB0
        half = xnT1 if i < 8 else xnT2
        loc = (i % 8) * 128
        war = ([e_xn[i - NB0], e_s1[i - NB0], e_s2[i - NB0]]
               if i >= NB0 else [])
        e_xdma[i] = dma("sync", xbuf[b][:, :], X[i * 128:(i + 1) * 128, :],
                        deps=war, stream=f"x{b}")
        e_s1[i] = vop("reduce_sum", s1t[b][:, :], xbuf[b][:, :], AX.X,
                      deps=[e_xdma[i]])
        e_sq[i] = act(sqbuf[:, :], xbuf[b][:, :], AF.Square,
                      deps=[e_xdma[i]] + ([e_s2[i - 1]] if i >= 1 else []))
        e_s2[i] = vop("reduce_sum", s2t[b][:, :], sqbuf[:, :], AX.X,
                      deps=[e_sq[i]])
        e_m = vop("tensor_scalar", mt[b][:, :], s1t[b][:, :], 1.0 / DM, None,
                  ALU.mult, deps=[e_s1[i]])
        e_m2 = vop("tensor_tensor", m2t[b][:, :], mt[b][:, :], mt[b][:, :],
                   ALU.mult, deps=[e_m])
        e_var = vop("scalar_tensor_tensor", vart[b][:, :], s2t[b][:, :],
                    1.0 / DM, m2t[b][:, :], ALU.mult, ALU.subtract,
                    deps=[e_s2[i], e_m2])
        e_sd = act(sdt[b][:, :], vart[b][:, :], AF.Sqrt, bias=LNEPS, deps=[e_var])
        e_rstd = vop("reciprocal", rstdt[b][:, :], sdt[b][:, :], deps=[e_sd])
        e_nm = vop("scalar_tensor_tensor", nmrs[b][:, :], mt[b][:, :], -1.0,
                   rstdt[b][:, :], ALU.mult, ALU.mult, deps=[e_rstd])
        e_xn[i] = act(xnbuf[b][:, :], xbuf[b][:, :], AF.Identity,
                      bias=nmrs[b][:, 0:1], scale=rstdt[b][:, 0:1],
                      deps=[e_nm, e_xdma[i]]
                      + ([e_tplast[i - NB0], e_fmm[i - NB0]]
                         if i >= NB0 else []))
        for k in range(NK):
            tb = k % 2
            war_t = (e_tpev.get((i, k - 2)) if k >= 2
                     else (e_tpev.get((i - 1, NK - 2 + k)) if i >= 1 else None))
            e_tp = transp(pstv[tb], xnbuf[b][:, k * 128:(k + 1) * 128],
                          identb[:, :], psum=PSM[2 + tb],
                          deps=[e_xn[i], e_idb, war_t])
            e_tpev[(i, k)] = act(half[k][:, loc:loc + 128], pstv[tb], AF.Copy,
                                 deps=[e_tp], psum_r=PSM[2 + tb])
            if k == NK - 1:
                e_tplast[i] = e_tp

        b2 = i % 2
        fv = pfv[b2]
        for k in range(NK):
            d0 = [e_tpev[(i, k)], e_wfb] if k == 0 else [e_tpev[(i, k)]]
            ee = mm(half[k][:, loc:loc + 128], wfb[:, k * 32:(k + 1) * 32], fv,
                    start=(k == 0), stop=(k == NK - 1), psum=PSM[b2],
                    deps=d0, inc=(sPE if k == NK - 1 else None))
        e_fmm[i] = ee
        e_fu[i] = vop("tensor_tensor", ubuf[b2][:, :], fv, bfbc[:, :], ALU.add,
                      deps=[e_fmm[i], e_bfbc] + ([e_a[i - 2]] if i >= 2 else []),
                      psum_r=PSM[b2])
        e_a[i] = act(abuf[b2][:, :], ubuf[b2][:, :], AF.Abs, deps=[e_fu[i]])
        e_e = act(ebuf[b2][:, :], abuf[b2][:, :], AF.Exp, scale=-1.0, deps=[e_a[i]])
        e_l[i] = act(lbuf[b2][:, :], ebuf[b2][:, :], AF.Ln, bias=1.0,
                     deps=[e_e] + ([e_f[i - 2]] if i >= 2 else []))
        e_mn = vop("tensor_scalar", mnbuf[b2][:, :], ubuf[b2][:, :], 0.0, None,
                   ALU.min, deps=[e_fu[i]])
        e_f[i] = vop("tensor_tensor", ftile[i][:, :], mnbuf[b2][:, :],
                     lbuf[b2][:, :], ALU.subtract, deps=[e_mn, e_l[i]])

    e_ftev = [None] * NSC
    for i in range(NSC):
        tb = i % 2
        war_t = e_ftev[i - 2] if i >= 2 else None
        e_tp = transp(pftv[tb], ftile[i][:, :], identf[:, :], psum=PSM[2 + tb],
                      deps=[e_f[i], e_idf, war_t])
        e_ftev[i] = vop("tensor_copy", fT[:, i * 128:(i + 1) * 128], pftv[tb],
                        deps=[e_tp], psum_r=PSM[2 + tb])
    e_scan = vop("tensor_tensor_scan", FcumT[:, :], fT[:, :], fT[:, :], 0.0,
                 ALU.add, ALU.bypass, deps=[e_ftev[NSC - 1]])
    e_fct = dma("sync", FCT, FcumT[:, :], deps=[e_scan], stream="fct")
    e_negF = [None] * NSC
    e_tpnf = None
    for j in range(NSC):
        tb = j % 2
        war_t = e_negF[j - 2] if j >= 2 else None
        e_tpnf = transp(pnfv[tb], FcumT[0:32, j * 128:(j + 1) * 128],
                        identf[0:32, 0:32], psum=PSM[2 + tb],
                        deps=[e_scan, war_t])
        e_negF[j] = vop("tensor_scalar", negF[j][:, :], pnfv[tb], -1.0, None,
                        ALU.mult, deps=[e_tpnf], psum_r=PSM[2 + tb])
    PH0BAR = [e_negF[NSC - 1], e_l[NSC - 1], e_tpnf, e_fct]

    e_tau = [None] * 3
    gstate = {"gi": 0, "red": {}, "prev": None}

    def gamma_steps():
        for w in range(3):
            e_z = vop("memset", gaccr[:, :], 0.0,
                      deps=[gstate["prev"] if w >= 1 else None])
            gstate["prev"] = e_z
            if w < 2:
                G = W1G if w == 0 else W2G
                tiles = [G[cf, hh].rearrange("p k m -> p (k m)")
                         for cf in range(NHF) for hh in range(2)]
            else:
                tiles = [W3G[dq, r].rearrange("p cc d -> p (cc d)")
                         for dq in range(4) for r in range(22)]
            for tile in tiles:
                gi = gstate["gi"]
                b = gi % 2
                e_g = dma("sync", gbuf[b][:, :], tile,
                          deps=[gstate["red"].get(gi - 2)],
                          stream=f"gam{b}")
                e_r = vop("reduce_sum", gpart[b][:, :], gbuf[b][:, :],
                          AX.X, deps=[e_g], apply_absolute_value=True)
                gstate["red"][gi] = e_r
                gstate["prev"] = vop("tensor_tensor", gaccr[:, :], gaccr[:, :],
                                     gpart[b][:, :], ALU.add,
                                     deps=[e_r, gstate["prev"]])
                gstate["gi"] += 1
                yield
            e_s = vop("tensor_reduce", gsum[:, :], gaccr[:, :], AX.C, ALU.add,
                      eng="gpsimd", deps=[gstate["prev"]])
            e_t = vop("tensor_scalar", gsum[:, :], gsum[:, :], 0.5 / (DM * HF),
                      0.5 * QEPS, ALU.mult, ALU.add, eng="gpsimd", deps=[e_s])
            e_w3 = dma("gpsimd", TAU[0:1, w:w + 1], gsum[:, :], deps=[e_t],
                       stream="tau")
            e_tb = dma("gpsimd", taubc[w][:, :],
                       TAU[0:1, w:w + 1].broadcast_to([128, 1]), deps=[e_w3],
                       stream="tau")
            e_tau[w] = vop("tensor_scalar_mul", taubcn[w][:, :], taubc[w][:, :],
                           -1.0, deps=[e_tb])
            gstate["prev"] = e_tau[w]

    _gammagen = gamma_steps()

    def gamma_drain(n):
        for _ in range(n):
            try:
                next(_gammagen)
            except StopIteration:
                return

    gamma_drain(110)

    ph0.close()
    ph1 = contextlib.ExitStack()
    E1 = ph1.enter_context
    wbuf = [E1(nc.sbuf_tensor(f"wbuf{i}", [128, 2048], BF16, side="right")) for i in range(2)]
    QTb = E1(nc.sbuf_tensor("QTb", [128, 2048], BF16, side="right"))
    KTb = E1(nc.sbuf_tensor("KTb", [128, 2048], BF16, side="right"))
    Vb = [E1(nc.sbuf_tensor(f"Vb{s}", [128, 130], BF16, side="right")) for s in range(NSC)]
    bcst = [E1(nc.sbuf_tensor(f"bcst{i}", [128, 2048], F32, side="right"))
            for i in range(2)]
    wstg = [E1(nc.sbuf_tensor(f"wstg{i}", [128, 512], F32, side="right"))
            for i in range(2)]
    ptb = [E1(nc.sbuf_tensor(f"ptb{i}", [128, 512], BF16, side="right")) for i in range(8)]
    OTraw = [E1(nc.sbuf_tensor("OTraw0", [64, 2048], BF16, side="right"))] * 2
    rdrows = E1(nc.sbuf_tensor("rdrows", [33, 2048], F32, side="right"))
    dnrow = rdrows[0:1, :]
    rden = [rdrows[32:33, :], rdrows[32:33, :]]
    denbc = [E1(nc.sbuf_tensor("denbc0", [64, 2048], BF16, side="right"))] * 2

    def xnt(k, n):
        half = xnT1 if n < 2 else xnT2
        return half[k][:, (n % 2) * 512:(n % 2) * 512 + 512]

    def xnt128(k, sc):
        half = xnT1 if sc < 8 else xnT2
        loc = (sc % 8) * 128
        return half[k][:, loc:loc + 128]

    e_wl = {}
    wstg_war = {}

    wl_qi = [0]

    def wload(G, c, dstbuf, war_deps):
        evs = None
        for hh in range(4):
            qi = wl_qi[0] % 2
            wl_qi[0] += 1
            wb2 = wstg[qi]
            e_l = dma("scalar", wb2[:, :],
                      G[c][:, hh * 4:(hh + 1) * 4, :].rearrange("p k m -> p (k m)"),
                      deps=[wstg_war.get(qi)]
                      + (war_deps if wstg_war.get(qi) is None else []),
                      stream=f"ws{qi}")
            evs = act(dstbuf[:, hh * 512:(hh + 1) * 512], wb2[:, :], AF.Copy,
                      deps=[e_l] + (war_deps if hh == 0 else []))
            wstg_war[qi] = evs
        return evs

    e_qk_done = {}
    e_vlast = {}
    e_pair_pv = {}
    e_head_done = {}
    e_bc = {}
    war_psO = {}
    stv_last = {}
    war_st = {}
    war_pt = {}
    e_rd_pr = {}

    for c in range(NPAIR):
        gamma_drain(10)
        wb_q = wbuf[c % 2]
        wb_k = wbuf[(c + 1) % 2]
        e_wl[(c, "q")] = wload(WQG, c, wb_q,
                               [e_qk_done.get(c - 1), e_wfb]
                               + (PH0BAR if c == 0 else []))
        e_wl[(c, "k")] = wload(WKG, c, wb_k,
                               [e_vlast.get(c - 1)]
                               + (PH0BAR if c == 0 else []))
        e_qmm = [None] * 4
        for k in range(NK):
            for n in range(4):
                d0 = [e_wl[(c, "q")]] if k == 0 else []
                e_qmm[n] = mm(wb_q[:, k * 128:(k + 1) * 128], xnt(k, n),
                              psm[n][:, :], start=(k == 0), stop=(k == NK - 1),
                              psum=PSM[n], deps=d0,
                              inc=(sPE if k == NK - 1 else None))
        e_qev = [act(QTb[:, n * 512:(n + 1) * 512], psm[n][:, :], AF.Identity,
                     scale=0.125,
                     deps=[e_qmm[n], e_pair_pv.get(c - 1)],
                     psum_r=PSM[n]) for n in range(4)]
        e_kmm = [None] * 4
        for k in range(NK):
            for n in range(4):
                d0 = [e_wl[(c, "k")]] if k == 0 else []
                e_kmm[n] = mm(wb_k[:, k * 128:(k + 1) * 128], xnt(k, n),
                              psm[n][:, :], start=(k == 0), stop=(k == NK - 1),
                              psum=PSM[n], deps=d0,
                              inc=(sPE if k == NK - 1 else None))
        e_kev = [act(KTb[:, n * 512:(n + 1) * 512], psm[n][:, :], AF.Copy,
                     deps=[e_kmm[n], e_pair_pv.get(c - 1)],
                     psum_r=PSM[n]) for n in range(4)]
        e_qk_done[c] = e_kmm[3]
        e_wl[(c, "v")] = wload(WVG, c, wb_q, [e_qmm[3]])
        psvv = [psm[2][:, 0:128], psm[3][:, 0:128]]
        e_vev = [None] * NSC
        for sc in range(NSC):
            tb = sc % 2
            e_vmm = None
            for k in range(NK):
                d0 = [e_wl[(c, "v")]] if k == 0 else []
                e_vmm = mm(xnt128(k, sc), wb_q[:, k * 128:(k + 1) * 128],
                           psvv[tb], start=(k == 0), stop=(k == NK - 1),
                           psum=PSM[2 + tb], deps=d0,
                           inc=(sPE if k == NK - 1 else None))
            e_ms = vop("memset", Vb[sc][:, :], 1.0,
                       deps=[e_pair_pv.get(c - 1)]
                       + (PH0BAR if c == 0 else []))
            vop("tensor_copy", Vb[sc][:, 0:64], psvv[tb][:, 0:64],
                deps=[e_vmm, e_ms])
            e_vev[sc] = vop("tensor_copy", Vb[sc][:, 65:129],
                            psvv[tb][:, 64:128],
                            deps=[e_vmm, e_ms], psum_r=PSM[2 + tb])
            if sc == NSC - 1:
                e_vlast[c] = e_vmm
        for hl in range(2):
            h = 2 * c + hl
            hb = h % 2
            e_bc[h] = dma("scalar", bcst[hb][:, :],
                          FCT[h:h + 1, :].broadcast_to([128, 2048]),
                          deps=[e_fct, stv_last.get(h - 2)]
                          + (PH0BAR if h <= 1 else []),
                          stream=f"bc{hb}")
            e_pv_last = None
            tiles = []
            for j in range(NSC):
                for bb in range(j // 4, 4):
                    cs = max(j * 128, bb * 512)
                    tiles.append((j, bb, cs, (bb + 1) * 512 - cs))
            GG = 4
            ngr = (len(tiles) + GG - 1) // GG
            tev = {}

            def emit_smm_group(g):
                for t in range(g * GG, min((g + 1) * GG, len(tiles))):
                    j, bb, cs, wdt = tiles[t]
                    sb = t % 4
                    p8 = t % 8
                    off = cs - bb * 512
                    pv_sc = psm[sb][:, off:off + wdt]
                    d0 = [e_kev[3]] if t == 0 else []
                    e_smm = mm(KTb[hl * 64:hl * 64 + 64,
                                   j * 128:(j + 1) * 128],
                               QTb[hl * 64:hl * 64 + 64, cs:cs + wdt],
                               pv_sc, start=True, stop=True, psum=PSM[sb],
                               deps=d0, inc=sPE)
                    e_stv = vop("scalar_tensor_tensor", pv_sc, pv_sc, 1.0,
                                bcst[hb][:, cs:cs + wdt], ALU.mult, ALU.add,
                                deps=[e_smm, e_bc[h]])
                    if cs == j * 128:
                        e_stv = vop("scalar_tensor_tensor",
                                    psm[sb][:, off:off + 128],
                                    psm[sb][:, off:off + 128], 1.0,
                                    negm[:, :], ALU.mult, ALU.add,
                                    deps=[e_stv, e_negm])
                    stv_last[h] = e_stv
                    e_p = act(ptb[p8][:, 0:wdt], pv_sc, AF.Exp,
                              bias=negF[j][:, h:h + 1],
                              deps=[e_stv, e_negF[j], war_pt.get(p8)],
                              psum_r=PSM[sb])
                    tev[t] = e_p

            def emit_pv_group(g):
                nonlocal e_pv_last
                for t in range(g * GG, min((g + 1) * GG, len(tiles))):
                    j, bb, cs, wdt = tiles[t]
                    p8 = t % 8
                    e_pv_last = mm(Vb[j][:, hl * 65:hl * 65 + 65],
                                   ptb[p8][:, 0:wdt],
                                   psO[0:65, cs:cs + wdt], start=(j == 0),
                                   stop=(j == min(15, 4 * bb + 3)), psum="psO",
                                   deps=[tev[t], e_vev[j]]
                                   + ([war_psO.get(0)] if t == 0 else []),
                                   inc=sPE)
                    war_pt[p8] = e_pv_last

            emit_smm_group(0)
            for g in range(1, ngr):
                emit_smm_group(g)
                emit_pv_group(g - 1)
            emit_pv_group(ngr - 1)
            e_pair_pv[c] = e_pv_last
            e_otr = act(OTraw[hb][:, :], psO[0:64, :], AF.Copy,
                        deps=[e_pv_last, e_head_done.get(h - 1)])
            war_psO_act = e_otr
            e_rd = vop("reciprocal", rden[hb], psO[64:65, :],
                       deps=[e_pv_last, e_rd_pr.get(0)], psum_r="psO")
            e_rw = dma("scalar", RDN[h:h + 1, :], rden[hb], deps=[e_rd], stream="rdn")
            e_rd_pr[0] = e_rw
            e_rb = dma("gpsimd", denbc[hb][:, :],
                       RDN[h:h + 1, :].broadcast_to([64, 2048]),
                       deps=[e_rw, e_head_done.get(h - 1)], stream="db")
            war_psO[0] = war_psO_act
            e_head_done[h] = vop("tensor_tensor",
                                 OT[c][hl * 64:hl * 64 + 64, :],
                                 OTraw[hb][:, :], denbc[hb][:, :], ALU.mult,
                                 deps=[e_otr, e_rb])

    PH1BAR = [e_head_done[H - 1], e_rd_pr[0], e_pair_pv[NPAIR - 1]]

    ph1.close()
    ph2 = contextlib.ExitStack()
    E2 = ph2.enter_context
    stgq = [E2(nc.sbuf_tensor(f"stgq{i}", [128, 1024], F32, side="right"))
            for i in range(2)]
    qoutb = [E2(nc.sbuf_tensor(f"qoutb{i}", [128, 1024], BF16, side="right"))
             for i in range(2)]
    gtb2 = [E2(nc.sbuf_tensor(f"gtb2{i}", [128, 1024], BF16, side="right"))
            for i in range(2)]
    ltb2 = [E2(nc.sbuf_tensor(f"ltb2{i}", [128, 1024], BF16, side="right"))
            for i in range(2)]

    e_qst = {}
    qp_state = {"qi": 0, "sub": {}, "store": {}, "lt": {}}

    def qpass_half(srcap, dstap, w, key, pre):
        qi = qp_state["qi"]
        qp_state["qi"] += 1
        buf = stgq[qi % 2]
        e_l = dma("sync", buf[:, :], srcap,
                  deps=pre + [qp_state["sub"].get(qi - 2), e_tau[w]],
                  stream=f"qp{qi % 2}")
        e_gt = vop("tensor_single_scalar", gtb2[qi % 2][:, :], buf[:, :],
                   taubc[w][:, 0:1], ALU.is_gt,
                   deps=[e_l, qp_state["sub"].get(qi - 2)])
        e_lt = vop("tensor_single_scalar", ltb2[qi % 2][:, :], buf[:, :],
                   taubcn[w][:, 0:1], ALU.is_lt,
                   deps=[e_l, qp_state["sub"].get(qi - 2)])
        e_sub = vop("tensor_tensor", qoutb[qi % 2][:, :], gtb2[qi % 2][:, :],
                    ltb2[qi % 2][:, :], ALU.subtract,
                    deps=[e_gt, e_lt, qp_state["store"].get(qi - 2)],
                    eng="gpsimd")
        qp_state["sub"][qi] = e_sub
        ev = dma("scalar", dstap, qoutb[qi % 2][:, :], deps=[e_sub],
                 stream=f"qs{qi % 2}")
        qp_state["store"][qi] = ev
        e_qst[key] = ev

    gamma_drain(1000)
    first = PH1BAR
    for cf in range(NHF):
        for w, G, D in ((0, W1G, Q1D), (1, W2G, Q2D)):
            for hh in range(2):
                qpass_half(G[cf, hh].rearrange("p k m -> p (k m)"),
                           D[cf][:, hh * 1024:hh * 1024 + 1024], w,
                           (w, cf), first)
                first = []
    for dq in range(4):
        for r4 in range(11):
            for hh in range(2):
                qpass_half(W3G[dq, r4 * 2 + hh].rearrange("p cc d -> p (cc d)"),
                           Q3D[dq, r4][:, hh * 1024:hh * 1024 + 1024], 2,
                           (2, dq, r4), [])

    wob = [E2(nc.sbuf_tensor(f"wob{c}", [128, 1024], BF16, side="right")) for c in range(NK)]
    wostg = [E2(nc.sbuf_tensor(f"wostg{i}", [128, 1024], F32, side="right"))
             for i in range(2)]
    attnb = [E2(nc.sbuf_tensor(f"attnb{i}", [128, 512], F32, side="right")) for i in range(4)]
    attn_war = {}
    e_dh_last = None
    e_attn_last = {}
    e_wo_pr = {}
    for dh in range(2):
        e_wo = []
        for cc in range(NK):
            wb2 = wostg[cc % 2]
            e_l = dma("scalar", wb2[:, :],
                      WOT[cc * 128:(cc + 1) * 128, dh * 1024:(dh + 1) * 1024],
                      deps=([e_dh_last] if dh == 1 and cc < 2 else [])
                      + [e_wo_pr.get(cc % 2)]
                      + (PH1BAR if dh == 0 and cc < 2 else []),
                      stream=f"ws{cc % 2}")
            ev = act(wob[cc][:, :], wb2[:, :], AF.Copy,
                     deps=[e_l] + ([e_dh_last] if dh == 1 else []))
            e_wo_pr[cc % 2] = ev
            e_wo.append(ev)
        for si in range(NSC):
            for bbo in range(2):
                pp = (si * 2 + bbo) % 4
                e_mmf = None
                for cc in range(NK):
                    d0 = [e_wo[cc]]
                    if cc == 0:
                        d0.append(e_head_done[H - 1])
                    e_mmf = mm(OT[cc][:, si * 128:(si + 1) * 128],
                               wob[cc][:, bbo * 512:(bbo + 1) * 512],
                               psm[pp][:, :], start=(cc == 0),
                               stop=(cc == NK - 1), psum=PSM[pp], deps=d0,
                               inc=(sPE if cc == NK - 1 else None))
                e_ev = act(attnb[pp][:, :], psm[pp][:, :], AF.Copy,
                           deps=[e_mmf, attn_war.get(pp)], psum_r=PSM[pp])
                e_attn_last[pp] = dma(
                    "sync",
                    ATTN[si * 128:(si + 1) * 128,
                         dh * 1024 + bbo * 512:dh * 1024 + bbo * 512 + 512],
                    attnb[pp][:, :], deps=[e_ev], stream=f"at{pp}")
                attn_war[pp] = e_attn_last[pp]
        e_dh_last = e_mmf
    e_ph2_done = [e_attn_last[pp] for pp in range(4)]

    ph2.close()
    phot.close()
    phg.close()
    ph3 = contextlib.ExitStack()
    E3 = ph3.enter_context
    hid = [E3(nc.sbuf_tensor(f"hid{cf}", [128, 1024], BF16, side="right")) for cf in range(NHF)]

    q1b = [E3(nc.sbuf_tensor(f"q1b{i}", [128, 2048], BF16, side="right"))
           for i in range(2)]
    q2b = [E3(nc.sbuf_tensor(f"q2b{i}", [128, 2048], BF16, side="right"))
           for i in range(2)]
    sgb = E3(nc.sbuf_tensor("sgb", [128, 512], F32, side="right"))


    e_w3ph = [None, None]
    e_hid = {}
    war_x = {}
    war_a = {}
    war_o = {}
    st3 = {"e_q3_pr": None, "last_y": None, "w3mm": None}

    def emit_x1x2(sh):
        xnTh = xnT1 if sh == 0 else xnT2
        e_q1 = e_q2 = None
        e_hev_pr = {}
        e_xmm_pr = {}
        e_xmm_by_cf = {}
        for cf in range(NHF):
            qp = cf % 2
            d_pre = (e_ph2_done if sh == 0 and cf == 0 else []) + [
                     e_w3ph[0] if sh == 1 and cf == 0 else None,
                     e_xmm_by_cf.get(cf - 2)]
            e_q1 = dma("sync", q1b[qp][:, :], Q1D[cf],
                       deps=d_pre + [e_qst[(0, cf)], st3.get("lq1")],
                       stream="w12a")
            st3["lq1"] = e_q1
            e_q2 = dma("sync", q2b[qp][:, :], Q2D[cf],
                       deps=[e_qst[(1, cf)], e_xmm_by_cf.get(cf - 2),
                             st3.get("lq2")], stream="w12b")
            st3["lq2"] = e_q2
            e_x1 = [None, None]
            e_x2 = [None, None]
            for k in range(NK):
                for n in range(4):
                    qq = q1b[qp] if n < 2 else q2b[qp]
                    eq = e_q1 if n < 2 else e_q2
                    d0 = []
                    if k == 0:
                        d0 = [eq]
                        if sh == 1 and cf == 0:
                            d0.append(e_w3ph[0])
                    ee = mm(qq[:, k * 128:(k + 1) * 128],
                            xnTh[k][:, (n % 2) * 512:(n % 2) * 512 + 512],
                            psm[n][:, :], start=(k == 0), stop=(k == NK - 1),
                            psum=PSM[n], deps=d0,
                            inc=(sPE if k == NK - 1 else None))
                    if n < 2:
                        e_x1[n] = ee
                    else:
                        e_x2[n - 2] = ee
                    if k == NK - 1:
                        e_xmm_pr[n] = ee
                        if n == 3:
                            e_xmm_by_cf[cf] = ee
            for n in range(2):
                e_sgm = act(sgb[:, 0:512], psm[n][:, :],
                            AF.Sigmoid,
                            deps=[e_x1[n],
                                  e_hev_pr.get((cf, 2) if n == 1
                                               else (cf - 1, 3))],
                            psum_r=PSM[n])
                e_hev_pr[(cf, n)] = e_sgm
                ph = vop("tensor_tensor", hid[cf][:, n * 512:(n + 1) * 512],
                         sgb[:, 0:512], psm[2 + n][:, :],
                         ALU.mult,
                         deps=[e_sgm, e_x2[n],
                               e_w3ph[0] if sh == 1 else None],
                         psum_r=PSM[2 + n])
                e_hev_pr[(cf, 2 + n)] = ph
                e_hid[(sh, cf)] = ph

    emit_x1x2(0)

    phx1.close()
    phnf.close()
    q3t = [E3(nc.sbuf_tensor(f"q3t{r}", [128, 2048], BF16)) for r in range(11)]
    xres = [E3(nc.sbuf_tensor(f"xres{i}", [128, 512], F32)) for i in range(2)]
    ares = [E3(nc.sbuf_tensor(f"ares{i}", [128, 512], F32)) for i in range(2)]

    def emit_w3(sh):
        for dq in range(4):
            e_q3cf = [None] * NHF
            for r4 in range(11):
                ev = dma("sync", q3t[r4][:, :], Q3D[dq, r4],
                         deps=[st3["e_q3_pr"] if r4 == 0 else None,
                               st3["w3mm"], e_qst[(2, dq, r4)],
                               st3.get("lq3"),
                               e_hid[(sh, NHF - 1)] if dq == 0 and r4 == 0
                               else None], stream="w12c")
                st3["lq3"] = ev
                for cf in range(r4 * 4, r4 * 4 + 4):
                    e_q3cf[cf] = ev
            st3["e_q3_pr"] = e_q3cf[NHF - 1]
            for sp in range(4):
                sweep0 = (sp == 0)
                pps = [(sp * 2) % 4, (sp * 2 + 1) % 4]
                e_mmf = [None, None]
                for cf in range(NHF):
                    for z in range(2):
                        si = sp * 2 + z
                        d0 = [e_q3cf[cf]] if sweep0 else []
                        if cf == 0:
                            d0 = d0 + [e_hid[(sh, NHF - 1)]]
                        e_mmf[z] = mm(hid[cf][:, si * 128:(si + 1) * 128],
                                      q3t[cf // 4][:, (cf % 4) * 512:
                                                   (cf % 4) * 512 + 512],
                                      psm[pps[z]][:, :], start=(cf == 0),
                                      stop=(cf == NHF - 1), psum=PSM[pps[z]],
                                      deps=d0,
                                      inc=(sPE if cf == NHF - 1 else None))
                st3["w3mm"] = e_mmf[1]
                for z in range(2):
                    si = sp * 2 + z
                    sig = sh * 8 + si
                    e_xr = dma("sync", xres[z][:, :],
                               X[sig * 128:(sig + 1) * 128,
                                 dq * 512:(dq + 1) * 512],
                               deps=[war_x.get(z),
                                     e_hid[(sh, NHF - 1)] if dq == 0
                                     and sp == 0 else None],
                               stream=f"x{z}")
                    e_ar = dma("gpsimd", ares[z][:, :],
                               ATTN[sig * 128:(sig + 1) * 128,
                                    dq * 512:(dq + 1) * 512],
                               deps=[war_a.get(z),
                                     e_hid[(sh, NHF - 1)] if dq == 0
                                     and sp == 0 else None]
                               + (e_ph2_done if sh == 0 and dq == 0
                                  and sp == 0 else []),
                               stream=f"ar{z}")
                    e_t1 = vop("tensor_tensor", xres[z][:, :], xres[z][:, :],
                               psm[pps[z]][:, :], ALU.add,
                               deps=[e_mmf[z], e_xr],
                               psum_r=PSM[pps[z]])
                    e_t2 = vop("tensor_tensor", xres[z][:, :], xres[z][:, :],
                               ares[z][:, :], ALU.add,
                               deps=[e_t1, e_ar])
                    war_a[z] = e_t2
                    st3["last_y"] = dma(
                        "scalar",
                        Y[sig * 128:(sig + 1) * 128, dq * 512:(dq + 1) * 512],
                        xres[z][:, :], deps=[e_t2], stream=f"y{z}")
                    war_x[z] = st3["last_y"]
        e_w3ph[sh] = st3["last_y"]

    emit_w3(0)
    emit_x1x2(1)
    emit_w3(1)

    n_out_per = 2 * 4 * 4
    P.op("sync", None, deps=[(stream_sem("y0"), 16 * n_out_per),
                             (stream_sem("y1"), 16 * n_out_per)])

    P.emit(nc, preamble=e_constv)
    ph3.close()
    st.close()
    return nc


_NC_CACHE = None


def kernel(L0_attn_hidden_states, L0_attn_Wq, L0_attn_Wk, L0_attn_Wv,
           L0_attn_Wf, L0_attn_bf, L0_attn_Wo, L0_ffn_w1, L0_ffn_w2, L0_ffn_w3):
    global _NC_CACHE
    hs = np.asarray(L0_attn_hidden_states)
    B = hs.shape[0]
    assert hs.shape == (8, S, DM)

    def colgroup_p(wt):
        return np.ascontiguousarray(
            wt.reshape(16, 128, 16, 128).transpose(2, 1, 0, 3))

    def w12group(wt):
        return np.ascontiguousarray(
            wt.reshape(2, 8, 128, 44, 128).transpose(3, 0, 2, 1, 4))

    def w3group(wt):
        return np.ascontiguousarray(
            wt.reshape(22, 2, 128, 4, 512).transpose(3, 0, 2, 1, 4))

    wqT = np.ascontiguousarray(np.asarray(L0_attn_Wq).T)
    wkT = np.ascontiguousarray(np.asarray(L0_attn_Wk).T)
    wvT = np.ascontiguousarray(np.asarray(L0_attn_Wv).T)
    wfT = np.ascontiguousarray(np.asarray(L0_attn_Wf).T)
    woT = np.ascontiguousarray(np.asarray(L0_attn_Wo).T)
    w1T = np.ascontiguousarray(np.asarray(L0_ffn_w1).T)
    w2T = np.ascontiguousarray(np.asarray(L0_ffn_w2).T)
    w3T = np.ascontiguousarray(np.asarray(L0_ffn_w3).T)
    common = {
        "wqg": colgroup_p(wqT),
        "wkg": colgroup_p(wkT),
        "wvg": colgroup_p(wvT),
        "wft": wfT,
        "bfv": np.asarray(L0_attn_bf).reshape(1, H).astype(np.float32),
        "wot": woT,
        "w1g": w12group(w1T),
        "w2g": w12group(w2T),
        "w3g": w3group(w3T),
        "idb": np.eye(128, dtype=bfdt),
        "idf": np.eye(128, dtype=np.float32),
        "negm": ((1.0 - np.triu(np.ones((128, 128), dtype=np.float32)))
                 * -1e9).astype(np.float32),
        "constv": np.tile(np.array([CONST_VALS], dtype=np.float32), (128, 1)),
    }
    in_maps = [dict(common, x=np.ascontiguousarray(hs[c])) for c in range(B)]
    if _NC_CACHE is None:
        _NC_CACHE = build_nc()
    res = run_bass_kernel_spmd(_NC_CACHE, in_maps, core_ids=list(range(B)))
    return np.stack([res.results[c]["y"] for c in range(B)], axis=0)


# version 14
# speedup vs baseline: 1.1292x; 1.0309x over previous
import contextlib
import numpy as np
import ml_dtypes
import concourse.bass as bass
import concourse.mybir as mybir
from concourse.bass_utils import run_bass_kernel_spmd

F32 = mybir.dt.float32
BF16 = mybir.dt.bfloat16
AF = mybir.ActivationFunctionType
ALU = mybir.AluOpType
AX = mybir.AxisListType
bfdt = ml_dtypes.bfloat16

S = 2048
DM = 2048
H = 32
HF = 5632
NK = DM // 128
NSC = S // 128
NPAIR = H // 2
NHF = HF // 128
QEPS = 1e-8
LNEPS = 1e-5

ENGINES = ("sync", "gpsimd", "scalar", "vector", "tensor")
CONST_VALS = [0.0, 1.0, -1.0, LNEPS, 0.125, 0.5, 2.0, 3.0]


class Prog:

    def __init__(self):
        self.ops = {e: [] for e in ENGINES}
        self.cnt = {}
        self.psum_dirty = {}
        self.pe_sat = None
        self.last_reader = {}

    def _bump(self, sem, by):
        c = self.cnt.get(sem.name, 0) + by
        self.cnt[sem.name] = c
        return (sem, c)

    def op(self, eng, emit, deps=(), inc=None, inc_by=1, psum_w=None, psum_r=None):
        deps = [d for d in deps if d is not None]
        if eng == "tensor" and psum_w is not None:
            lr = self.last_reader.pop(psum_w, None)
            if lr is not None:
                deps.append(lr)
            if self.psum_dirty.get(psum_w) and not deps:
                assert self.pe_sat is not None, "matmul separator unavailable"
                deps.append(self.pe_sat)
        if eng == "tensor" and deps:
            for k in self.psum_dirty:
                self.psum_dirty[k] = False
            self.pe_sat = deps[-1]
        if eng == "tensor" and psum_w is not None:
            self.psum_dirty[psum_w] = True
        dd = {}
        for (sem, val) in deps:
            if val > dd.get(sem.name, (None, -1))[1]:
                dd[sem.name] = (sem, val)
        deps = tuple(dd.values())
        ev = None
        if inc is not None:
            ev = self._bump(inc, inc_by)
        if psum_r is not None:
            assert ev is not None
            for nm in (psum_r if isinstance(psum_r, (list, tuple)) else [psum_r]):
                self.last_reader[nm] = ev
        self.ops[eng].append((emit, deps, ((inc, inc_by),) if inc is not None else ()))
        return ev

    def emit(self, nc, preamble=None):
        with nc.Block() as blk:
            for eng_name in ENGINES:
                oplist = self.ops[eng_name]
                if preamble is not None and eng_name != "sync":
                    oplist = [(None, (preamble,), ())] + oplist

                def make(oplist):
                    def fn(eng):
                        for (emit, deps, incs) in oplist:
                            for (sem, val) in deps:
                                eng.wait_ge(sem, val)
                            if emit is None:
                                continue
                            inst = emit(eng)
                            for (sem, by) in incs:
                                inst.then_inc(sem, by)
                    return fn

                getattr(blk, eng_name)(make(oplist))


def build_nc(debug=False):
    nc = bass.Bass("TRN2", target_bir_lowering=False,
                   dynamic_dma_scratch_size=16384)

    X = nc.dram_tensor("x", [S, DM], F32, kind="ExternalInput").ap()
    WQG = nc.dram_tensor("wqg", [NPAIR, 128, NK, 128], F32, kind="ExternalInput").ap()
    WKG = nc.dram_tensor("wkg", [NPAIR, 128, NK, 128], F32, kind="ExternalInput").ap()
    WVG = nc.dram_tensor("wvg", [NPAIR, 128, NK, 128], F32, kind="ExternalInput").ap()
    WFT = nc.dram_tensor("wft", [DM, H], F32, kind="ExternalInput").ap()
    BF = nc.dram_tensor("bfv", [1, H], F32, kind="ExternalInput").ap()
    WOT = nc.dram_tensor("wot", [DM, DM], F32, kind="ExternalInput").ap()
    W1G = nc.dram_tensor("w1g", [NHF, 2, 128, 8, 128], F32, kind="ExternalInput").ap()
    W2G = nc.dram_tensor("w2g", [NHF, 2, 128, 8, 128], F32, kind="ExternalInput").ap()
    W3G = nc.dram_tensor("w3g", [4, 22, 128, 2, 512], F32, kind="ExternalInput").ap()
    IDB = nc.dram_tensor("idb", [128, 128], BF16, kind="ExternalInput").ap()
    IDF = nc.dram_tensor("idf", [128, 128], F32, kind="ExternalInput").ap()
    NEGM = nc.dram_tensor("negm", [128, 128], F32, kind="ExternalInput").ap()
    CONSTV = nc.dram_tensor("constv", [128, 8], F32, kind="ExternalInput").ap()
    Y = nc.dram_tensor("y", [S, DM], F32, kind="ExternalOutput").ap()
    dbg = dict(kind="ExternalOutput") if debug else {}
    ATTN = nc.dram_tensor("attn_scr", [S, DM], F32, **dbg).ap()
    FCT = nc.dram_tensor("fct_scr", [H, S], F32, **dbg).ap()
    RDN = nc.dram_tensor("rdn_scr", [H, S], F32, **dbg).ap()
    GP = nc.dram_tensor("gp_scr", [3, 128], F32).ap()
    TAU = nc.dram_tensor("tau_scr", [1, 4], F32).ap()
    Q1D = nc.dram_tensor("q1_scr", [NHF, 128, 2048], BF16).ap()
    Q2D = nc.dram_tensor("q2_scr", [NHF, 128, 2048], BF16).ap()
    Q3D = nc.dram_tensor("q3_scr", [4, 11, 128, 2048], BF16).ap()

    P = Prog()
    st = contextlib.ExitStack()
    E = st.enter_context

    sDs = E(nc.semaphore("sDs"))
    sDg = E(nc.semaphore("sDg"))
    sDc = E(nc.semaphore("sDc"))
    sPE = E(nc.semaphore("sPE"))
    sA = E(nc.semaphore("sA"))
    sV = E(nc.semaphore("sV"))
    sG = E(nc.semaphore("sG"))
    sOUT = E(nc.semaphore("sOUT"))

    SEMS = {}

    def stream_sem(name):
        if name not in SEMS:
            SEMS[name] = E(nc.semaphore("sdma_" + name))
        return SEMS[name]

    def dma(q, out, in_, deps=(), stream=None, slow=False):
        assert stream is not None
        s = stream_sem(stream)

        def em(e, out=out, in_=in_, slow=slow):
            if slow:
                return e.dma_start(out=out, in_=in_, allow_slow_non_contiguous=True)
            return e.dma_start(out=out, in_=in_)
        return P.op(q, em, deps=deps, inc=s, inc_by=16)

    def mm(lhsT, rhs, out, start, stop, psum, deps=(), inc=None):
        def em(e, out=out, lhsT=lhsT, rhs=rhs, start=start, stop=stop):
            return e.matmul(out, lhsT, rhs, start=start, stop=stop)
        return P.op("tensor", em, deps=deps, inc=inc, psum_w=psum)

    def transp(out, in_, ident, psum, deps=(), inc=sPE):
        def em(e, out=out, in_=in_, ident=ident):
            return e.transpose(out, in_, ident)
        return P.op("tensor", em, deps=deps, inc=inc, psum_w=psum)

    def act(out, in_, func, bias=0.0, scale=1.0, deps=(), psum_r=None):
        def em(e, out=out, in_=in_, func=func, bias=bias, scale=scale):
            return e.activation(out=out, in_=in_, func=func, bias=bias, scale=scale)
        return P.op("scalar", em, deps=deps, inc=sA, psum_r=psum_r)

    def vop(name, *args, deps=(), psum_r=None, eng="vector", **kw):
        def em(e, name=name, args=args, kw=kw):
            return getattr(e, name)(*args, **kw)
        return P.op(eng, em, deps=deps, inc=(sG if eng == "gpsimd" else sV),
                    psum_r=psum_r)

    xnT1 = None
    xnT2 = [E(nc.sbuf_tensor(f"xnT2_{k}", [128, 1024], BF16)) for k in range(NK)]
    identb = E(nc.sbuf_tensor("identb", [128, 128], BF16))
    negm = E(nc.sbuf_tensor("negm_sb", [128, 128], F32))
    taubc = [E(nc.sbuf_tensor(f"taubc{w}", [128, 1], F32)) for w in range(3)]
    taubcn = [E(nc.sbuf_tensor(f"taubcn{w}", [128, 1], F32)) for w in range(3)]
    constv = E(nc.sbuf_tensor("constv_sb", [128, 8], F32))
    phg = contextlib.ExitStack()
    Eg = phg.enter_context
    gbuf = [Eg(nc.sbuf_tensor(f"gbuf{i}", [128, 1024], F32, side="right")) for i in range(2)]
    gpart = [Eg(nc.sbuf_tensor(f"gpart{i}", [128, 1], F32, side="right"))
             for i in range(2)]
    gaccr = Eg(nc.sbuf_tensor("gaccr", [128, 1], F32, side="right"))
    gsum = Eg(nc.sbuf_tensor("gsum", [1, 1], F32, side="right"))
    phnf = contextlib.ExitStack()
    negF = [phnf.enter_context(nc.sbuf_tensor(f"negF{j}", [128, 32], F32))
            for j in range(NSC)]
    phot = contextlib.ExitStack()
    OT = [phot.enter_context(nc.sbuf_tensor(f"OT{c}", [128, 2048], BF16, side="right"))
          for c in range(NPAIR)]
    phx1 = contextlib.ExitStack()

    psm = [E(nc.psum_tensor(f"psm{i}", [128, 512], F32)) for i in range(4)]
    psO = E(nc.psum_tensor("psO", [65, 2048], F32))
    PSM = [f"psm{i}" for i in range(4)]

    xnT1 = [phx1.enter_context(nc.sbuf_tensor(f"xnT1_{k}", [128, 1024], BF16))
            for k in range(NK)]

    e_constv = dma("sync", constv[:, :], CONSTV, stream="const")
    for i, v in enumerate(CONST_VALS):
        try:
            nc.const_aps.register(F32, v, constv[:, i:i + 1])
        except Exception:
            pass
    dma("sync", identb[:, :], IDB, stream="const")
    dma("sync", negm[:, :], NEGM, stream="const")

    ph0 = contextlib.ExitStack()
    E0 = ph0.enter_context
    NB0 = 3
    identf = E0(nc.sbuf_tensor("identf", [128, 128], F32, side="right"))
    xbuf = [E0(nc.sbuf_tensor(f"xbuf{i}", [128, 2048], F32, side="right")) for i in range(NB0)]
    sqbuf = E0(nc.sbuf_tensor("sqbuf", [128, 2048], F32, side="right"))
    xnbuf = [E0(nc.sbuf_tensor(f"xnbuf{i}", [128, 2048], BF16, side="right")) for i in range(NB0)]
    s1t = [E0(nc.sbuf_tensor(f"s1t{i}", [128, 1], F32, side="right")) for i in range(NB0)]
    s2t = [E0(nc.sbuf_tensor(f"s2t{i}", [128, 1], F32, side="right")) for i in range(NB0)]
    mt = [E0(nc.sbuf_tensor(f"mt{i}", [128, 1], F32, side="right")) for i in range(NB0)]
    m2t = [E0(nc.sbuf_tensor(f"m2t{i}", [128, 1], F32, side="right")) for i in range(NB0)]
    vart = [E0(nc.sbuf_tensor(f"vart{i}", [128, 1], F32, side="right")) for i in range(NB0)]
    sdt = [E0(nc.sbuf_tensor(f"sdt{i}", [128, 1], F32, side="right")) for i in range(NB0)]
    rstdt = [E0(nc.sbuf_tensor(f"rstdt{i}", [128, 1], F32, side="right")) for i in range(NB0)]
    nmrs = [E0(nc.sbuf_tensor(f"nmrs{i}", [128, 1], F32, side="right")) for i in range(NB0)]
    ftile = [E0(nc.sbuf_tensor(f"ftile{i}", [128, 32], F32, side="right")) for i in range(NSC)]
    fT = E0(nc.sbuf_tensor("fT", [32, 2048], F32, side="right"))
    FcumT = E0(nc.sbuf_tensor("FcumT", [32, 2048], F32, side="right"))
    ubuf = [E0(nc.sbuf_tensor(f"ubuf{i}", [128, 32], F32, side="right")) for i in range(2)]
    abuf = [E0(nc.sbuf_tensor(f"abuf{i}", [128, 32], F32, side="right")) for i in range(2)]
    ebuf = [E0(nc.sbuf_tensor(f"ebuf{i}", [128, 32], F32, side="right")) for i in range(2)]
    lbuf = [E0(nc.sbuf_tensor(f"lbuf{i}", [128, 32], F32, side="right")) for i in range(2)]
    mnbuf = [E0(nc.sbuf_tensor(f"mnbuf{i}", [128, 32], F32, side="right")) for i in range(2)]
    wfb = E0(nc.sbuf_tensor("wfb", [128, 512], BF16, side="right"))
    bfbc = E0(nc.sbuf_tensor("bfbc", [128, 32], F32, side="right"))

    dma("sync", identf[:, :], IDF, stream="const")
    e_wfb = dma("gpsimd", wfb[:, :].rearrange("p (k h) -> p k h", k=NK),
                WFT.rearrange("(k p) h -> p k h", p=128), stream="wfb")
    E_CONST = dma("sync", bfbc[:, :], BF.broadcast_to([128, H]), stream="const")
    e_bfbc = E_CONST
    e_idb = e_idf = e_triu = e_negm = E_CONST

    pstv = [psm[2][:, 0:64].bitcast(BF16), psm[3][:, 0:64].bitcast(BF16)]
    pftv = [psm[2][0:32, 128:256], psm[3][0:32, 128:256]]
    pnfv = [psm[2][:, 256:288], psm[3][:, 256:288]]
    pfv = [psm[0][:, 0:32], psm[1][:, 0:32]]

    e_xn = [None] * NSC
    e_xdma = [None] * NSC
    e_s1 = [None] * NSC
    e_sq = [None] * NSC
    e_s2 = [None] * NSC
    e_f = [None] * NSC
    e_fu = [None] * NSC
    e_fmm = [None] * NSC
    e_a = [None] * NSC
    e_l = [None] * NSC
    e_tpev = {}
    e_tplast = [None] * NSC

    for i in range(NSC):
        b = i % NB0
        half = xnT1 if i < 8 else xnT2
        loc = (i % 8) * 128
        war = ([e_xn[i - NB0], e_s1[i - NB0], e_s2[i - NB0]]
               if i >= NB0 else [])
        e_xdma[i] = dma("sync", xbuf[b][:, :], X[i * 128:(i + 1) * 128, :],
                        deps=war, stream=f"x{b}")
        e_s1[i] = vop("reduce_sum", s1t[b][:, :], xbuf[b][:, :], AX.X,
                      deps=[e_xdma[i]])
        e_sq[i] = act(sqbuf[:, :], xbuf[b][:, :], AF.Square,
                      deps=[e_xdma[i]] + ([e_s2[i - 1]] if i >= 1 else []))
        e_s2[i] = vop("reduce_sum", s2t[b][:, :], sqbuf[:, :], AX.X,
                      deps=[e_sq[i]])
        e_m = vop("tensor_scalar", mt[b][:, :], s1t[b][:, :], 1.0 / DM, None,
                  ALU.mult, deps=[e_s1[i]])
        e_m2 = vop("tensor_tensor", m2t[b][:, :], mt[b][:, :], mt[b][:, :],
                   ALU.mult, deps=[e_m])
        e_var = vop("scalar_tensor_tensor", vart[b][:, :], s2t[b][:, :],
                    1.0 / DM, m2t[b][:, :], ALU.mult, ALU.subtract,
                    deps=[e_s2[i], e_m2])
        e_sd = act(sdt[b][:, :], vart[b][:, :], AF.Sqrt, bias=LNEPS, deps=[e_var])
        e_rstd = vop("reciprocal", rstdt[b][:, :], sdt[b][:, :], deps=[e_sd])
        e_nm = vop("scalar_tensor_tensor", nmrs[b][:, :], mt[b][:, :], -1.0,
                   rstdt[b][:, :], ALU.mult, ALU.mult, deps=[e_rstd])
        e_xn[i] = act(xnbuf[b][:, :], xbuf[b][:, :], AF.Identity,
                      bias=nmrs[b][:, 0:1], scale=rstdt[b][:, 0:1],
                      deps=[e_nm, e_xdma[i]]
                      + ([e_tplast[i - NB0], e_fmm[i - NB0]]
                         if i >= NB0 else []))
        for k in range(NK):
            tb = k % 2
            war_t = (e_tpev.get((i, k - 2)) if k >= 2
                     else (e_tpev.get((i - 1, NK - 2 + k)) if i >= 1 else None))
            e_tp = transp(pstv[tb], xnbuf[b][:, k * 128:(k + 1) * 128],
                          identb[:, :], psum=PSM[2 + tb],
                          deps=[e_xn[i], e_idb, war_t])
            e_tpev[(i, k)] = act(half[k][:, loc:loc + 128], pstv[tb], AF.Copy,
                                 deps=[e_tp], psum_r=PSM[2 + tb])
            if k == NK - 1:
                e_tplast[i] = e_tp

        b2 = i % 2
        fv = pfv[b2]
        for k in range(NK):
            d0 = [e_tpev[(i, k)], e_wfb] if k == 0 else [e_tpev[(i, k)]]
            ee = mm(half[k][:, loc:loc + 128], wfb[:, k * 32:(k + 1) * 32], fv,
                    start=(k == 0), stop=(k == NK - 1), psum=PSM[b2],
                    deps=d0, inc=(sPE if k == NK - 1 else None))
        e_fmm[i] = ee
        e_fu[i] = vop("tensor_tensor", ubuf[b2][:, :], fv, bfbc[:, :], ALU.add,
                      deps=[e_fmm[i], e_bfbc] + ([e_a[i - 2]] if i >= 2 else []),
                      psum_r=PSM[b2])
        e_a[i] = act(abuf[b2][:, :], ubuf[b2][:, :], AF.Abs, deps=[e_fu[i]])
        e_e = act(ebuf[b2][:, :], abuf[b2][:, :], AF.Exp, scale=-1.0, deps=[e_a[i]])
        e_l[i] = act(lbuf[b2][:, :], ebuf[b2][:, :], AF.Ln, bias=1.0,
                     deps=[e_e] + ([e_f[i - 2]] if i >= 2 else []))
        e_mn = vop("tensor_scalar", mnbuf[b2][:, :], ubuf[b2][:, :], 0.0, None,
                   ALU.min, deps=[e_fu[i]])
        e_f[i] = vop("tensor_tensor", ftile[i][:, :], mnbuf[b2][:, :],
                     lbuf[b2][:, :], ALU.subtract, deps=[e_mn, e_l[i]])

    e_ftev = [None] * NSC
    for i in range(NSC):
        tb = i % 2
        war_t = e_ftev[i - 2] if i >= 2 else None
        e_tp = transp(pftv[tb], ftile[i][:, :], identf[:, :], psum=PSM[2 + tb],
                      deps=[e_f[i], e_idf, war_t])
        e_ftev[i] = vop("tensor_copy", fT[:, i * 128:(i + 1) * 128], pftv[tb],
                        deps=[e_tp], psum_r=PSM[2 + tb])
    e_scan = vop("tensor_tensor_scan", FcumT[:, :], fT[:, :], fT[:, :], 0.0,
                 ALU.add, ALU.bypass, deps=[e_ftev[NSC - 1]])
    e_fct = dma("sync", FCT, FcumT[:, :], deps=[e_scan], stream="fct")
    e_negF = [None] * NSC
    e_tpnf = None
    for j in range(NSC):
        tb = j % 2
        war_t = e_negF[j - 2] if j >= 2 else None
        e_tpnf = transp(pnfv[tb], FcumT[0:32, j * 128:(j + 1) * 128],
                        identf[0:32, 0:32], psum=PSM[2 + tb],
                        deps=[e_scan, war_t])
        e_negF[j] = vop("tensor_scalar", negF[j][:, :], pnfv[tb], -1.0, None,
                        ALU.mult, deps=[e_tpnf], psum_r=PSM[2 + tb])
    PH0BAR = [e_negF[NSC - 1], e_l[NSC - 1], e_tpnf, e_fct]

    e_tau = [None] * 3
    gstate = {"gi": 0, "red": {}, "prev": None}

    def gamma_steps():
        for w in range(3):
            e_z = vop("memset", gaccr[:, :], 0.0,
                      deps=[gstate["prev"] if w >= 1 else None])
            gstate["prev"] = e_z
            if w < 2:
                G = W1G if w == 0 else W2G
                tiles = [G[cf, hh].rearrange("p k m -> p (k m)")
                         for cf in range(NHF) for hh in range(2)]
            else:
                tiles = [W3G[dq, r].rearrange("p cc d -> p (cc d)")
                         for dq in range(4) for r in range(22)]
            for tile in tiles:
                gi = gstate["gi"]
                b = gi % 2
                e_g = dma("sync", gbuf[b][:, :], tile,
                          deps=[gstate["red"].get(gi - 2)],
                          stream=f"gam{b}")
                e_r = vop("reduce_sum", gpart[b][:, :], gbuf[b][:, :],
                          AX.X, deps=[e_g], apply_absolute_value=True)
                gstate["red"][gi] = e_r
                gstate["prev"] = vop("tensor_tensor", gaccr[:, :], gaccr[:, :],
                                     gpart[b][:, :], ALU.add,
                                     deps=[e_r, gstate["prev"]])
                gstate["gi"] += 1
                yield
            e_s = vop("tensor_reduce", gsum[:, :], gaccr[:, :], AX.C, ALU.add,
                      eng="gpsimd", deps=[gstate["prev"]])
            e_t = vop("tensor_scalar", gsum[:, :], gsum[:, :], 0.5 / (DM * HF),
                      0.5 * QEPS, ALU.mult, ALU.add, eng="gpsimd", deps=[e_s])
            e_w3 = dma("gpsimd", TAU[0:1, w:w + 1], gsum[:, :], deps=[e_t],
                       stream="tau")
            e_tb = dma("gpsimd", taubc[w][:, :],
                       TAU[0:1, w:w + 1].broadcast_to([128, 1]), deps=[e_w3],
                       stream="tau")
            e_tau[w] = vop("tensor_scalar_mul", taubcn[w][:, :], taubc[w][:, :],
                           -1.0, deps=[e_tb])
            gstate["prev"] = e_tau[w]

    _gammagen = gamma_steps()

    def gamma_drain(n):
        for _ in range(n):
            try:
                next(_gammagen)
            except StopIteration:
                return

    gamma_drain(110)

    ph0.close()
    ph1 = contextlib.ExitStack()
    E1 = ph1.enter_context
    wbuf = [E1(nc.sbuf_tensor(f"wbuf{i}", [128, 2048], BF16, side="right")) for i in range(2)]
    QTb = E1(nc.sbuf_tensor("QTb", [128, 2048], BF16, side="right"))
    KTb = E1(nc.sbuf_tensor("KTb", [128, 2048], BF16, side="right"))
    Vb = [E1(nc.sbuf_tensor(f"Vb{s}", [128, 130], BF16, side="right")) for s in range(NSC)]
    bcst = [E1(nc.sbuf_tensor(f"bcst{i}", [128, 2048], F32, side="right"))
            for i in range(2)]
    wstg = [E1(nc.sbuf_tensor(f"wstg{i}", [128, 512], F32, side="right"))
            for i in range(2)]
    ptb = [E1(nc.sbuf_tensor(f"ptb{i}", [128, 512], BF16, side="right")) for i in range(8)]
    OTraw = [E1(nc.sbuf_tensor("OTraw0", [64, 2048], BF16, side="right"))] * 2
    rdrows = E1(nc.sbuf_tensor("rdrows", [33, 2048], F32, side="right"))
    dnrow = rdrows[0:1, :]
    rden = [rdrows[32:33, :], rdrows[32:33, :]]
    denbc = [E1(nc.sbuf_tensor("denbc0", [64, 2048], BF16, side="right"))] * 2

    def xnt(k, n):
        half = xnT1 if n < 2 else xnT2
        return half[k][:, (n % 2) * 512:(n % 2) * 512 + 512]

    def xnt128(k, sc):
        half = xnT1 if sc < 8 else xnT2
        loc = (sc % 8) * 128
        return half[k][:, loc:loc + 128]

    e_wl = {}
    wstg_war = {}

    wl_qi = [0]

    def wload(G, c, dstbuf, war_deps):
        evs = None
        for hh in range(4):
            qi = wl_qi[0] % 2
            wl_qi[0] += 1
            wb2 = wstg[qi]
            e_l = dma("scalar", wb2[:, :],
                      G[c][:, hh * 4:(hh + 1) * 4, :].rearrange("p k m -> p (k m)"),
                      deps=[wstg_war.get(qi)]
                      + (war_deps if wstg_war.get(qi) is None else []),
                      stream=f"ws{qi}")
            evs = act(dstbuf[:, hh * 512:(hh + 1) * 512], wb2[:, :], AF.Copy,
                      deps=[e_l] + (war_deps if hh == 0 else []))
            wstg_war[qi] = evs
        return evs

    e_qk_done = {}
    e_vlast = {}
    e_pair_pv = {}
    e_head_done = {}
    e_bc = {}
    war_psO = {}
    stv_last = {}
    war_st = {}
    war_pt = {}
    e_rd_pr = {}

    for c in range(NPAIR):
        gamma_drain(10)
        wb_q = wbuf[c % 2]
        wb_k = wbuf[(c + 1) % 2]
        if c == 0:
            e_wl[(0, "q")] = wload(WQG, 0, wb_q, [e_wfb] + PH0BAR)
            e_wl[(0, "k")] = wload(WKG, 0, wb_k, PH0BAR)
        e_qmm = [None] * 4
        for k in range(NK):
            for n in range(4):
                d0 = [e_wl[(c, "q")]] if k == 0 else []
                e_qmm[n] = mm(wb_q[:, k * 128:(k + 1) * 128], xnt(k, n),
                              psm[n][:, :], start=(k == 0), stop=(k == NK - 1),
                              psum=PSM[n], deps=d0,
                              inc=(sPE if k == NK - 1 else None))
        e_qev = [act(QTb[:, n * 512:(n + 1) * 512], psm[n][:, :], AF.Identity,
                     scale=0.125,
                     deps=[e_qmm[n], e_pair_pv.get(c - 1)],
                     psum_r=PSM[n]) for n in range(4)]
        e_kmm = [None] * 4
        for k in range(NK):
            for n in range(4):
                d0 = [e_wl[(c, "k")]] if k == 0 else []
                e_kmm[n] = mm(wb_k[:, k * 128:(k + 1) * 128], xnt(k, n),
                              psm[n][:, :], start=(k == 0), stop=(k == NK - 1),
                              psum=PSM[n], deps=d0,
                              inc=(sPE if k == NK - 1 else None))
        e_kev = [act(KTb[:, n * 512:(n + 1) * 512], psm[n][:, :], AF.Copy,
                     deps=[e_kmm[n], e_pair_pv.get(c - 1)],
                     psum_r=PSM[n]) for n in range(4)]
        e_qk_done[c] = e_kmm[3]
        e_wl[(c, "v")] = wload(WVG, c, wb_q, [e_qmm[3]])
        psvv = [psm[2][:, 0:128], psm[3][:, 0:128]]
        e_vev = [None] * NSC
        for sc in range(NSC):
            tb = sc % 2
            e_vmm = None
            for k in range(NK):
                d0 = [e_wl[(c, "v")]] if k == 0 else []
                e_vmm = mm(xnt128(k, sc), wb_q[:, k * 128:(k + 1) * 128],
                           psvv[tb], start=(k == 0), stop=(k == NK - 1),
                           psum=PSM[2 + tb], deps=d0,
                           inc=(sPE if k == NK - 1 else None))
            e_ms = vop("memset", Vb[sc][:, :], 1.0,
                       deps=[e_pair_pv.get(c - 1)]
                       + (PH0BAR if c == 0 else []))
            vop("tensor_copy", Vb[sc][:, 0:64], psvv[tb][:, 0:64],
                deps=[e_vmm, e_ms])
            e_vev[sc] = vop("tensor_copy", Vb[sc][:, 65:129],
                            psvv[tb][:, 64:128],
                            deps=[e_vmm, e_ms], psum_r=PSM[2 + tb])
            if sc == NSC - 1:
                e_vlast[c] = e_vmm
        if c + 1 < NPAIR:
            e_wl[(c + 1, "q")] = wload(WQG, c + 1, wbuf[(c + 1) % 2],
                                       [e_qk_done.get(c)])
            e_wl[(c + 1, "k")] = wload(WKG, c + 1, wbuf[c % 2],
                                       [e_vlast.get(c)])
        for hl in range(2):
            h = 2 * c + hl
            hb = h % 2
            e_bc[h] = dma("scalar", bcst[hb][:, :],
                          FCT[h:h + 1, :].broadcast_to([128, 2048]),
                          deps=[e_fct, stv_last.get(h - 2)]
                          + (PH0BAR if h <= 1 else []),
                          stream=f"bc{hb}")
            e_pv_last = None
            tiles = []
            for j in range(NSC):
                for bb in range(j // 4, 4):
                    cs = max(j * 128, bb * 512)
                    tiles.append((j, bb, cs, (bb + 1) * 512 - cs))
            GG = 4
            ngr = (len(tiles) + GG - 1) // GG
            tev = {}

            def emit_smm_group(g):
                for t in range(g * GG, min((g + 1) * GG, len(tiles))):
                    j, bb, cs, wdt = tiles[t]
                    sb = t % 4
                    p8 = t % 8
                    off = cs - bb * 512
                    pv_sc = psm[sb][:, off:off + wdt]
                    d0 = [e_kev[3]] if t == 0 else []
                    e_smm = mm(KTb[hl * 64:hl * 64 + 64,
                                   j * 128:(j + 1) * 128],
                               QTb[hl * 64:hl * 64 + 64, cs:cs + wdt],
                               pv_sc, start=True, stop=True, psum=PSM[sb],
                               deps=d0, inc=sPE)
                    e_stv = vop("scalar_tensor_tensor", pv_sc, pv_sc, 1.0,
                                bcst[hb][:, cs:cs + wdt], ALU.mult, ALU.add,
                                deps=[e_smm, e_bc[h]])
                    if cs == j * 128:
                        e_stv = vop("scalar_tensor_tensor",
                                    psm[sb][:, off:off + 128],
                                    psm[sb][:, off:off + 128], 1.0,
                                    negm[:, :], ALU.mult, ALU.add,
                                    deps=[e_stv, e_negm])
                    stv_last[h] = e_stv
                    e_p = act(ptb[p8][:, 0:wdt], pv_sc, AF.Exp,
                              bias=negF[j][:, h:h + 1],
                              deps=[e_stv, e_negF[j], war_pt.get(p8)],
                              psum_r=PSM[sb])
                    tev[t] = e_p

            def emit_pv_group(g):
                nonlocal e_pv_last
                for t in range(g * GG, min((g + 1) * GG, len(tiles))):
                    j, bb, cs, wdt = tiles[t]
                    p8 = t % 8
                    e_pv_last = mm(Vb[j][:, hl * 65:hl * 65 + 65],
                                   ptb[p8][:, 0:wdt],
                                   psO[0:65, cs:cs + wdt], start=(j == 0),
                                   stop=(j == min(15, 4 * bb + 3)), psum="psO",
                                   deps=[tev[t], e_vev[j]]
                                   + ([war_psO.get(0)] if t == 0 else []),
                                   inc=sPE)
                    war_pt[p8] = e_pv_last

            emit_smm_group(0)
            for g in range(1, ngr):
                emit_smm_group(g)
                emit_pv_group(g - 1)
            emit_pv_group(ngr - 1)
            e_pair_pv[c] = e_pv_last
            e_otr = act(OTraw[hb][:, :], psO[0:64, :], AF.Copy,
                        deps=[e_pv_last, e_head_done.get(h - 1)])
            war_psO_act = e_otr
            e_rd = vop("reciprocal", rden[hb], psO[64:65, :],
                       deps=[e_pv_last, e_rd_pr.get(0)], psum_r="psO")
            e_rw = dma("scalar", RDN[h:h + 1, :], rden[hb], deps=[e_rd], stream="rdn")
            e_rd_pr[0] = e_rw
            e_rb = dma("gpsimd", denbc[hb][:, :],
                       RDN[h:h + 1, :].broadcast_to([64, 2048]),
                       deps=[e_rw, e_head_done.get(h - 1)], stream="db")
            war_psO[0] = war_psO_act
            e_head_done[h] = vop("tensor_tensor",
                                 OT[c][hl * 64:hl * 64 + 64, :],
                                 OTraw[hb][:, :], denbc[hb][:, :], ALU.mult,
                                 deps=[e_otr, e_rb])

    PH1BAR = [e_head_done[H - 1], e_rd_pr[0], e_pair_pv[NPAIR - 1]]

    ph1.close()
    ph2 = contextlib.ExitStack()
    E2 = ph2.enter_context
    stgq = [E2(nc.sbuf_tensor(f"stgq{i}", [128, 1024], F32, side="right"))
            for i in range(2)]
    qoutb = [E2(nc.sbuf_tensor(f"qoutb{i}", [128, 1024], BF16, side="right"))
             for i in range(2)]
    gtb2 = [E2(nc.sbuf_tensor(f"gtb2{i}", [128, 1024], BF16, side="right"))
            for i in range(2)]
    ltb2 = [E2(nc.sbuf_tensor(f"ltb2{i}", [128, 1024], BF16, side="right"))
            for i in range(2)]

    e_qst = {}
    qp_state = {"qi": 0, "sub": {}, "store": {}, "lt": {}}

    def qpass_half(srcap, dstap, w, key, pre):
        qi = qp_state["qi"]
        qp_state["qi"] += 1
        buf = stgq[qi % 2]
        e_l = dma("sync", buf[:, :], srcap,
                  deps=pre + [qp_state["sub"].get(qi - 2), e_tau[w]],
                  stream=f"qp{qi % 2}")
        e_gt = vop("tensor_single_scalar", gtb2[qi % 2][:, :], buf[:, :],
                   taubc[w][:, 0:1], ALU.is_gt,
                   deps=[e_l, qp_state["sub"].get(qi - 2)])
        e_lt = vop("tensor_single_scalar", ltb2[qi % 2][:, :], buf[:, :],
                   taubcn[w][:, 0:1], ALU.is_lt,
                   deps=[e_l, qp_state["sub"].get(qi - 2)])
        e_sub = vop("tensor_tensor", qoutb[qi % 2][:, :], gtb2[qi % 2][:, :],
                    ltb2[qi % 2][:, :], ALU.subtract,
                    deps=[e_gt, e_lt, qp_state["store"].get(qi - 2)],
                    eng="gpsimd")
        qp_state["sub"][qi] = e_sub
        ev = dma("scalar", dstap, qoutb[qi % 2][:, :], deps=[e_sub],
                 stream=f"qs{qi % 2}")
        qp_state["store"][qi] = ev
        e_qst[key] = ev

    gamma_drain(1000)
    first = PH1BAR
    for cf in range(NHF):
        for w, G, D in ((0, W1G, Q1D), (1, W2G, Q2D)):
            for hh in range(2):
                qpass_half(G[cf, hh].rearrange("p k m -> p (k m)"),
                           D[cf][:, hh * 1024:hh * 1024 + 1024], w,
                           (w, cf), first)
                first = []
    for dq in range(4):
        for r4 in range(11):
            for hh in range(2):
                qpass_half(W3G[dq, r4 * 2 + hh].rearrange("p cc d -> p (cc d)"),
                           Q3D[dq, r4][:, hh * 1024:hh * 1024 + 1024], 2,
                           (2, dq, r4), [])

    wob = [E2(nc.sbuf_tensor(f"wob{c}", [128, 1024], BF16, side="right")) for c in range(NK)]
    wostg = [E2(nc.sbuf_tensor(f"wostg{i}", [128, 1024], F32, side="right"))
             for i in range(2)]
    attnb = [E2(nc.sbuf_tensor(f"attnb{i}", [128, 512], F32, side="right")) for i in range(4)]
    attn_war = {}
    e_dh_last = None
    e_attn_last = {}
    e_wo_pr = {}
    for dh in range(2):
        e_wo = []
        for cc in range(NK):
            wb2 = wostg[cc % 2]
            e_l = dma("scalar", wb2[:, :],
                      WOT[cc * 128:(cc + 1) * 128, dh * 1024:(dh + 1) * 1024],
                      deps=([e_dh_last] if dh == 1 and cc < 2 else [])
                      + [e_wo_pr.get(cc % 2)]
                      + (PH1BAR if dh == 0 and cc < 2 else []),
                      stream=f"ws{cc % 2}")
            ev = act(wob[cc][:, :], wb2[:, :], AF.Copy,
                     deps=[e_l] + ([e_dh_last] if dh == 1 else []))
            e_wo_pr[cc % 2] = ev
            e_wo.append(ev)
        for si in range(NSC):
            for bbo in range(2):
                pp = (si * 2 + bbo) % 4
                e_mmf = None
                for cc in range(NK):
                    d0 = [e_wo[cc]]
                    if cc == 0:
                        d0.append(e_head_done[H - 1])
                    e_mmf = mm(OT[cc][:, si * 128:(si + 1) * 128],
                               wob[cc][:, bbo * 512:(bbo + 1) * 512],
                               psm[pp][:, :], start=(cc == 0),
                               stop=(cc == NK - 1), psum=PSM[pp], deps=d0,
                               inc=(sPE if cc == NK - 1 else None))
                e_ev = act(attnb[pp][:, :], psm[pp][:, :], AF.Copy,
                           deps=[e_mmf, attn_war.get(pp)], psum_r=PSM[pp])
                e_attn_last[pp] = dma(
                    "sync",
                    ATTN[si * 128:(si + 1) * 128,
                         dh * 1024 + bbo * 512:dh * 1024 + bbo * 512 + 512],
                    attnb[pp][:, :], deps=[e_ev], stream=f"at{pp}")
                attn_war[pp] = e_attn_last[pp]
        e_dh_last = e_mmf
    e_ph2_done = [e_attn_last[pp] for pp in range(4)]

    ph2.close()
    phot.close()
    phg.close()
    ph3 = contextlib.ExitStack()
    E3 = ph3.enter_context
    hid = [E3(nc.sbuf_tensor(f"hid{cf}", [128, 1024], BF16, side="right")) for cf in range(NHF)]

    q1b = [E3(nc.sbuf_tensor(f"q1b{i}", [128, 2048], BF16, side="right"))
           for i in range(2)]
    q2b = [E3(nc.sbuf_tensor(f"q2b{i}", [128, 2048], BF16, side="right"))
           for i in range(2)]
    sgb = E3(nc.sbuf_tensor("sgb", [128, 512], F32, side="right"))


    e_w3ph = [None, None]
    e_hid = {}
    war_x = {}
    war_a = {}
    war_o = {}
    st3 = {"e_q3_pr": None, "last_y": None, "w3mm": None}

    def emit_x1x2(sh):
        xnTh = xnT1 if sh == 0 else xnT2
        e_q1 = e_q2 = None
        e_hev_pr = {}
        e_xmm_pr = {}
        e_xmm_by_cf = {}
        for cf in range(NHF):
            qp = cf % 2
            d_pre = (e_ph2_done if sh == 0 and cf == 0 else []) + [
                     e_w3ph[0] if sh == 1 and cf == 0 else None,
                     e_xmm_by_cf.get(cf - 2)]
            e_q1h = [None, None]
            e_q2h = [None, None]
            for hh in range(2):
                e_q1h[hh] = dma("sync",
                                q1b[qp][:, hh * 1024:(hh + 1) * 1024],
                                Q1D[cf][:, hh * 1024:(hh + 1) * 1024],
                                deps=(d_pre if hh == 0 else [])
                                + [e_qst[(0, cf)], st3.get("lq1")],
                                stream="w12a")
                st3["lq1"] = e_q1h[hh]
                e_q2h[hh] = dma("sync",
                                q2b[qp][:, hh * 1024:(hh + 1) * 1024],
                                Q2D[cf][:, hh * 1024:(hh + 1) * 1024],
                                deps=([e_xmm_by_cf.get(cf - 2)] if hh == 0
                                      else [])
                                + [e_qst[(1, cf)], st3.get("lq2")],
                                stream="w12b")
                st3["lq2"] = e_q2h[hh]
            e_x1 = [None, None]
            e_x2 = [None, None]
            for k in range(NK):
                for n in range(4):
                    qq = q1b[qp] if n < 2 else q2b[qp]
                    eq = e_q1h[k // 8] if n < 2 else e_q2h[k // 8]
                    d0 = []
                    if k == 0 or k == 8:
                        d0 = [eq]
                        if sh == 1 and cf == 0 and k == 0:
                            d0.append(e_w3ph[0])
                    ee = mm(qq[:, k * 128:(k + 1) * 128],
                            xnTh[k][:, (n % 2) * 512:(n % 2) * 512 + 512],
                            psm[n][:, :], start=(k == 0), stop=(k == NK - 1),
                            psum=PSM[n], deps=d0,
                            inc=(sPE if k == NK - 1 else None))
                    if n < 2:
                        e_x1[n] = ee
                    else:
                        e_x2[n - 2] = ee
                    if k == NK - 1:
                        e_xmm_pr[n] = ee
                        if n == 3:
                            e_xmm_by_cf[cf] = ee
            for n in range(2):
                e_sgm = act(sgb[:, 0:512], psm[n][:, :],
                            AF.Sigmoid,
                            deps=[e_x1[n],
                                  e_hev_pr.get((cf, 2) if n == 1
                                               else (cf - 1, 3))],
                            psum_r=PSM[n])
                e_hev_pr[(cf, n)] = e_sgm
                ph = vop("tensor_tensor", hid[cf][:, n * 512:(n + 1) * 512],
                         sgb[:, 0:512], psm[2 + n][:, :],
                         ALU.mult,
                         deps=[e_sgm, e_x2[n],
                               e_w3ph[0] if sh == 1 else None],
                         psum_r=PSM[2 + n])
                e_hev_pr[(cf, 2 + n)] = ph
                e_hid[(sh, cf)] = ph

    emit_x1x2(0)

    phx1.close()
    phnf.close()
    q3t = [E3(nc.sbuf_tensor(f"q3t{r}", [128, 2048], BF16)) for r in range(11)]
    xres = [E3(nc.sbuf_tensor(f"xres{i}", [128, 512], F32)) for i in range(2)]
    ares = [E3(nc.sbuf_tensor(f"ares{i}", [128, 512], F32)) for i in range(2)]

    def emit_w3(sh):
        for dq in range(4):
            e_q3cf = [None] * NHF
            for r4 in range(11):
                ev = dma("sync", q3t[r4][:, :], Q3D[dq, r4],
                         deps=[st3["e_q3_pr"] if r4 == 0 else None,
                               st3["w3mm"], e_qst[(2, dq, r4)],
                               st3.get("lq3"),
                               e_hid[(sh, NHF - 1)] if dq == 0 and r4 == 0
                               else None], stream="w12c")
                st3["lq3"] = ev
                for cf in range(r4 * 4, r4 * 4 + 4):
                    e_q3cf[cf] = ev
            st3["e_q3_pr"] = e_q3cf[NHF - 1]
            for sp in range(4):
                sweep0 = (sp == 0)
                pps = [(sp * 2) % 4, (sp * 2 + 1) % 4]
                e_mmf = [None, None]
                for cf in range(NHF):
                    for z in range(2):
                        si = sp * 2 + z
                        d0 = [e_q3cf[cf]] if sweep0 else []
                        if cf == 0:
                            d0 = d0 + [e_hid[(sh, NHF - 1)]]
                        e_mmf[z] = mm(hid[cf][:, si * 128:(si + 1) * 128],
                                      q3t[cf // 4][:, (cf % 4) * 512:
                                                   (cf % 4) * 512 + 512],
                                      psm[pps[z]][:, :], start=(cf == 0),
                                      stop=(cf == NHF - 1), psum=PSM[pps[z]],
                                      deps=d0,
                                      inc=(sPE if cf == NHF - 1 else None))
                st3["w3mm"] = e_mmf[1]
                for z in range(2):
                    si = sp * 2 + z
                    sig = sh * 8 + si
                    e_xr = dma("sync", xres[z][:, :],
                               X[sig * 128:(sig + 1) * 128,
                                 dq * 512:(dq + 1) * 512],
                               deps=[war_x.get(z),
                                     e_hid[(sh, NHF - 1)] if dq == 0
                                     and sp == 0 else None],
                               stream=f"x{z}")
                    e_ar = dma("gpsimd", ares[z][:, :],
                               ATTN[sig * 128:(sig + 1) * 128,
                                    dq * 512:(dq + 1) * 512],
                               deps=[war_a.get(z),
                                     e_hid[(sh, NHF - 1)] if dq == 0
                                     and sp == 0 else None]
                               + (e_ph2_done if sh == 0 and dq == 0
                                  and sp == 0 else []),
                               stream=f"ar{z}")
                    e_t1 = vop("tensor_tensor", xres[z][:, :], xres[z][:, :],
                               psm[pps[z]][:, :], ALU.add,
                               deps=[e_mmf[z], e_xr],
                               psum_r=PSM[pps[z]])
                    e_t2 = vop("tensor_tensor", xres[z][:, :], xres[z][:, :],
                               ares[z][:, :], ALU.add,
                               deps=[e_t1, e_ar])
                    war_a[z] = e_t2
                    st3["last_y"] = dma(
                        "scalar",
                        Y[sig * 128:(sig + 1) * 128, dq * 512:(dq + 1) * 512],
                        xres[z][:, :], deps=[e_t2], stream=f"y{z}")
                    war_x[z] = st3["last_y"]
        e_w3ph[sh] = st3["last_y"]

    emit_w3(0)
    emit_x1x2(1)
    emit_w3(1)

    n_out_per = 2 * 4 * 4
    P.op("sync", None, deps=[(stream_sem("y0"), 16 * n_out_per),
                             (stream_sem("y1"), 16 * n_out_per)])

    P.emit(nc, preamble=e_constv)
    ph3.close()
    st.close()
    return nc


_NC_CACHE = None


def kernel(L0_attn_hidden_states, L0_attn_Wq, L0_attn_Wk, L0_attn_Wv,
           L0_attn_Wf, L0_attn_bf, L0_attn_Wo, L0_ffn_w1, L0_ffn_w2, L0_ffn_w3):
    global _NC_CACHE
    hs = np.asarray(L0_attn_hidden_states)
    B = hs.shape[0]
    assert hs.shape == (8, S, DM)

    def colgroup_p(wt):
        return np.ascontiguousarray(
            wt.reshape(16, 128, 16, 128).transpose(2, 1, 0, 3))

    def w12group(wt):
        return np.ascontiguousarray(
            wt.reshape(2, 8, 128, 44, 128).transpose(3, 0, 2, 1, 4))

    def w3group(wt):
        return np.ascontiguousarray(
            wt.reshape(22, 2, 128, 4, 512).transpose(3, 0, 2, 1, 4))

    wqT = np.ascontiguousarray(np.asarray(L0_attn_Wq).T)
    wkT = np.ascontiguousarray(np.asarray(L0_attn_Wk).T)
    wvT = np.ascontiguousarray(np.asarray(L0_attn_Wv).T)
    wfT = np.ascontiguousarray(np.asarray(L0_attn_Wf).T)
    woT = np.ascontiguousarray(np.asarray(L0_attn_Wo).T)
    w1T = np.ascontiguousarray(np.asarray(L0_ffn_w1).T)
    w2T = np.ascontiguousarray(np.asarray(L0_ffn_w2).T)
    w3T = np.ascontiguousarray(np.asarray(L0_ffn_w3).T)
    common = {
        "wqg": colgroup_p(wqT),
        "wkg": colgroup_p(wkT),
        "wvg": colgroup_p(wvT),
        "wft": wfT,
        "bfv": np.asarray(L0_attn_bf).reshape(1, H).astype(np.float32),
        "wot": woT,
        "w1g": w12group(w1T),
        "w2g": w12group(w2T),
        "w3g": w3group(w3T),
        "idb": np.eye(128, dtype=bfdt),
        "idf": np.eye(128, dtype=np.float32),
        "negm": ((1.0 - np.triu(np.ones((128, 128), dtype=np.float32)))
                 * -1e9).astype(np.float32),
        "constv": np.tile(np.array([CONST_VALS], dtype=np.float32), (128, 1)),
    }
    in_maps = [dict(common, x=np.ascontiguousarray(hs[c])) for c in range(B)]
    if _NC_CACHE is None:
        _NC_CACHE = build_nc()
    res = run_bass_kernel_spmd(_NC_CACHE, in_maps, core_ids=list(range(B)))
    return np.stack([res.results[c]["y"] for c in range(B)], axis=0)
